# version 21
# baseline (speedup 1.0000x reference)
import numpy as np
import ml_dtypes

import concourse.bass as bass
import concourse.mybir as mybir
from concourse.bass_utils import run_bass_kernel_spmd

F32 = mybir.dt.float32
BF16 = mybir.dt.bfloat16
I8 = mybir.dt.int8
I32 = mybir.dt.int32
ALU = mybir.AluOpType
ACT = mybir.ActivationFunctionType
AX = mybir.AxisListType.X
BDT = ml_dtypes.bfloat16

DM = 2048
NH = 16
HD = 128
LAT = 512
ITER = 8192
SEQ = 4096
NB = 4
TOK = 2048
NKG = 8
NOG = 4

DEBUG_TAPS = ()
PHASES = "ABCDE"

XPOSE_CHAIN = False
EPS = 1e-5
NEG = -10000.0


def wpad(tt):
    return 256 * (tt + 1)


class Plan:

    ENGS = ("T", "V", "S", "G", "Q")
    NPOOL = 8

    def __init__(self):
        self.ops = {e: [] for e in self.ENGS}
        self.cnt = {k: 0 for k in ("T", "V", "S", "G")}
        self.qn = {}
        for q in ("dQ", "dA", "dG"):
            for i in range(self.NPOOL):
                self.cnt[f"{q}{i}"] = 0
            self.qn[q] = 0

    def step(self, eng, fn, waits=(), dma=None, xpose=False):
        ws = []
        for w in waits:
            if w is None:
                continue
            ws.append(w)
        if dma:
            sem = f"{dma}{self.qn[dma] % self.NPOOL}"
            self.qn[dma] += 1
            if self.cnt[sem] > 0:
                ws.append((sem, self.cnt[sem]))
            inc = 16
        else:
            sem = eng
            inc = 1
        if xpose and XPOSE_CHAIN:
            if getattr(self, "last_xpose", None) is not None:
                ws.append(self.last_xpose)
        self.ops[eng].append((tuple(ws), fn, sem, inc))
        self.cnt[sem] += inc
        ev = (sem, self.cnt[sem])
        if xpose:
            self.last_xpose = ev
        return ev

    def barrier(self):
        marks = dict(self.cnt)
        evs = tuple(marks.items())
        for e in self.ENGS:
            self.ops[e].append((evs, None, None, 0))

    def replay(self, eng_handle, eng_name, sems):
        cur = {}
        for ws, fn, sem, inc in self.ops[eng_name]:
            for s, v in ws:
                if cur.get(s, 0) < v:
                    eng_handle.wait_ge(sems[s], v)
                    cur[s] = v
            if fn is None:
                continue
            inst = fn(eng_handle)
            if inc:
                inst.then_inc(sems[sem], inc)


class Slot:

    __slots__ = ("ready", "users")

    def __init__(self):
        self.ready = []
        self.users = []

    def produced(self, ev):
        self.ready = list(ev) if isinstance(ev, list) else [ev]
        self.users = []

    def used(self, ev):
        self.users.append(ev)


def prod_waits(*slots):
    ws = []
    for s in slots:
        ws.extend(s.users)
        ws.extend(s.ready)
    return ws


def cons_waits(*slots):
    ws = []
    for s in slots:
        ws.extend(s.ready)
    return ws


def build_program():
    nc = bass.Bass("TRN2", target_bir_lowering=False)

    xin = nc.dram_tensor("xin", [SEQ, DM], F32, kind="ExternalInput").ap()
    wq = nc.dram_tensor("wq", [DM, DM], F32, kind="ExternalInput").ap()
    wdn = nc.dram_tensor("wdn", [DM, LAT], F32, kind="ExternalInput").ap()
    wku = nc.dram_tensor("wku", [LAT, DM], F32, kind="ExternalInput").ap()
    wvu = nc.dram_tensor("wvu", [LAT, DM], F32, kind="ExternalInput").ap()
    wo = nc.dram_tensor("wo", [DM, DM], F32, kind="ExternalInput").ap()
    wg8 = nc.dram_tensor("wg8", [DM, ITER], I8, kind="ExternalInput").ap()
    wu8 = nc.dram_tensor("wu8", [DM, ITER], I8, kind="ExternalInput").ap()
    wd8 = nc.dram_tensor("wd8", [ITER, DM], I8, kind="ExternalInput").ap()
    bqv = nc.dram_tensor("bqv", [128, NH], F32, kind="ExternalInput").ap()
    bob = nc.dram_tensor("bob", [128, DM], F32, kind="ExternalInput").ap()
    sgv = nc.dram_tensor("sgv", [128, 64], F32, kind="ExternalInput").ap()
    suv = nc.dram_tensor("suv", [128, 64], F32, kind="ExternalInput").ap()
    sdb = nc.dram_tensor("sdb", [128, DM], F32, kind="ExternalInput").ap()
    ckh = nc.dram_tensor("ckh", [128, SEQ], F32, kind="ExternalInput").ap()
    ckl = nc.dram_tensor("ckl", [128, SEQ], F32, kind="ExternalInput").ap()
    skh = nc.dram_tensor("skh", [128, SEQ], F32, kind="ExternalInput").ap()
    skl = nc.dram_tensor("skl", [128, SEQ], F32, kind="ExternalInput").ap()
    cqh = nc.dram_tensor("cqh", [128, TOK], F32, kind="ExternalInput").ap()
    cql = nc.dram_tensor("cql", [128, TOK], F32, kind="ExternalInput").ap()
    sqh = nc.dram_tensor("sqh", [128, TOK], F32, kind="ExternalInput").ap()
    sql = nc.dram_tensor("sql", [128, TOK], F32, kind="ExternalInput").ap()
    mks = nc.dram_tensor("mks", [128, NH * 256], F32, kind="ExternalInput").ap()
    idx = nc.dram_tensor("idx", [1, 64], I32, kind="ExternalInput").ap()

    out = nc.dram_tensor("out", [TOK, DM], F32, kind="ExternalOutput").ap()

    def scratch(name, shape, dtype):
        kind = "ExternalOutput" if name in DEBUG_TAPS else "Internal"
        return nc.dram_tensor(name, shape, dtype, kind=kind).ap()

    k_scr = scratch("k_scr", [DM, SEQ], BF16)
    v_scr = scratch("v_scr", [DM, SEQ], BF16)
    q_scr = scratch("q_scr", [DM, TOK], BF16)
    xnf_scr = scratch("xnf_scr", [DM, SEQ], BF16)
    x1_scr = scratch("x1_scr", [TOK, DM], F32)
    wo_scr = scratch("wo_scr", [DM, DM], BF16)
    ck_scr = scratch("ck_scr", [128, SEQ], F32)
    sk_scr = scratch("sk_scr", [128, SEQ], F32)
    cq_scr = scratch("cq_scr", [128, TOK], F32)
    sq_scr2 = scratch("sq_scr2", [128, TOK], F32)
    x1n_scr = scratch("x1n_scr", [TOK, DM], BF16)

    plan = Plan()
    from contextlib import ExitStack
    stack = ExitStack()
    e = stack.enter_context

    ARF_U = 12800
    ARB_U = 45056
    CTX_U = 32768
    arF = e(nc.sbuf_tensor("arF", [128, ARF_U], F32))
    arB = e(nc.sbuf_tensor("arB", [128, ARB_U], BF16))
    ctxr = e(nc.sbuf_tensor("ctxr", [128, CTX_U], BF16))
    smalls = e(nc.sbuf_tensor("smalls", [128, 256], F32))
    idx_s = e(nc.sbuf_tensor("idx_s", [1, 64], I32))
    sqs = e(nc.sbuf_tensor("sqs", [128, DM], BF16)).ap()

    def ar_view(tensor, off, shape):
        n = int(np.prod(shape))
        ap = tensor.ap()[:, off:off + n]
        if len(shape) == 2:
            ap = ap.rearrange("p (a b) -> p a b", b=shape[1])
        elif len(shape) == 3:
            ap = ap.rearrange("p (a b c) -> p a b c", b=shape[1], c=shape[2])
        return ap

    class Carver:
        def __init__(self, tensor, total):
            self.t = tensor
            self.total = total
            self.off = 0

        def take(self, shape, dtype=None):
            n = int(np.prod(shape))
            assert self.off + n <= self.total, (self.off, n, self.total)
            ap = ar_view(self.t, self.off, shape)
            self.off += n
            return ap

    sm_off = [0]

    def small(shape):
        n = int(np.prod(shape))
        ap = smalls.ap()[:, sm_off[0]:sm_off[0] + n]
        if len(shape) == 2:
            ap = ap.rearrange("p (a b) -> p a b", b=shape[1])
        sm_off[0] += n
        return ap

    eps_s = small([1])
    bq_s = small([NH])
    sg_s = small([64])
    su_s = small([64])
    st_s = small([16])
    lacc = small([32])
    lsum = small([4])
    negc = small([4])
    rl_s = small([4])
    assert sm_off[0] <= 256

    ps = [e(nc.psum_tensor(f"ps{i}", [128, 512], F32)) for i in range(8)]
    ps_ap = [p.ap() for p in ps]
    ps_slot = [Slot() for _ in range(8)]

    sems = {}
    sem_names = ["T", "V", "S", "G"] + [f"{q}{i}" for q in ("dQ", "dA", "dG")
                                        for i in range(Plan.NPOOL)]
    for name in sem_names:
        sems[name] = e(nc.semaphore(name))

    scr_slot = {n: Slot() for n in ("k", "v", "q", "xnf", "x1", "x1n")}

    st = plan.step

    ev = st("V", lambda v: v.memset(eps_s, EPS))
    ev_idx = st("Q", lambda q: q.dma_start(idx_s[:], idx), dma="dQ")
    st("Q", lambda q: q.dma_start(bq_s, bqv), dma="dQ")
    st("Q", lambda q: q.dma_start(sg_s, sgv), dma="dQ")
    ev_bias = st("Q", lambda q: q.dma_start(su_s, suv), dma="dQ")

    if "A" in PHASES:
        cf = Carver(arF, ARF_U)
        x_t = [cf.take([DM]) for _ in range(3)]
        rot = [cf.take([512]) for _ in range(2)]
        m1 = [cf.take([512]) for _ in range(2)]
        cslc = [cf.take([512]) for _ in range(2)]
        sslc = [cf.take([512]) for _ in range(2)]
        tb1 = cf.take([1024])
        tb2 = cf.take([1024])
        ca = Carver(arB, ARB_U)
        wdn_s = ca.take([NH, LAT])
        wku_s = ca.take([4, DM])
        wvu_s = ca.take([4, DM])
        xnfm0 = ca.take([NH, 512])
        xnfm1 = ca.take([NH, 512])
        xnfm2 = [xnfm0, xnfm1]
        kvl2 = [ca.take([4, 512]) for _ in range(2)]
        cc = Carver(ctxr, CTX_U)
        xn_tm = [cc.take([DM]) for _ in range(2)]
        kst = cc.take([NH, 512])
        vst = cc.take([NH, 512])
        assert cc.off <= CTX_U

        s_cslc = [Slot(), Slot()]
        s_sslc = [Slot(), Slot()]
        s_tb1, s_tb2 = Slot(), Slot()
        s_xnfm2 = [Slot(), Slot()]
        s_kvl2 = [Slot(), Slot()]
        s_xt = [Slot(), Slot(), Slot()]
        s_xntm = [Slot(), Slot()]
        s_kst, s_vst = Slot(), Slot()
        s_rot = [Slot(), Slot()]
        s_m1 = [Slot(), Slot()]
        s_stats = Slot()
        s_wdn, s_wku, s_wvu = Slot(), Slot(), Slot()

        def wload(dst_ap, src, p_stride, nchunk, width, slot):
            def fn(g, dst_ap=dst_ap, src=src):
                sap = bass.AP(src.tensor, 0,
                              [[p_stride, 128], [128 * p_stride, nchunk],
                               [1, width]])
                return g.dma_start(dst_ap, sap)
            slot.produced(st("G", fn, dma="dG"))

        wload(wdn_s, wdn, LAT, NH, LAT, s_wdn)
        wload(wku_s, wku, DM, 4, DM, s_wku)
        wload(wvu_s, wvu, DM, 4, DM, s_wvu)

        tbl_slots = {}

        def build_tbl(hi, lo, dst_scr, width):
            slot = Slot()
            tbl_slots[dst_scr.tensor.name] = slot
            wevs = []
            for s0 in range(0, width, 1024):
                e1 = st("Q", lambda q, hi=hi, s0=s0: q.dma_start(
                    tb1, hi[:, s0:s0 + 1024]), waits=prod_waits(s_tb1),
                    dma="dQ")
                s_tb1.produced(e1)
                e2 = st("Q", lambda q, lo=lo, s0=s0: q.dma_start(
                    tb2, lo[:, s0:s0 + 1024]), waits=prod_waits(s_tb2),
                    dma="dQ")
                s_tb2.produced(e2)
                ev = st("V", lambda v: v.tensor_add(tb1, tb1, tb2),
                        waits=[e1, e2])
                s_tb1.produced(ev)
                s_tb2.used(ev)
                ew = st("Q", lambda q, d=dst_scr, s0=s0: q.dma_start(
                    d[:, s0:s0 + 1024], tb1), waits=[ev], dma="dQ")
                s_tb1.used(ew)
                wevs.append(ew)
            slot.produced(wevs)

        build_tbl(ckh, ckl, ck_scr, SEQ)
        build_tbl(skh, skl, sk_scr, SEQ)
        build_tbl(cqh, cql, cq_scr, TOK)
        build_tbl(sqh, sql, sq_scr2, TOK)

        for og in range(4):
            def fnc(g, og=og):
                sap = bass.AP(wo.tensor, og * 512,
                              [[DM, 128], [128 * DM, NH], [1, 512]])
                return g.dma_start(xnfm1, sap)
            ev = st("G", fnc, waits=prod_waits(s_xnfm2[1]), dma="dG")
            s_xnfm2[1].produced(ev)

            def fnw(q, og=og):
                dap = bass.AP(wo_scr.tensor, og * 512,
                              [[DM, 128], [128 * DM, NH], [1, 512]])
                return q.dma_start(dap, xnfm1)
            ev = st("Q", fnw, waits=cons_waits(s_xnfm2[1]), dma="dQ")
            s_xnfm2[1].used(ev)

        def front(kg):
            kgp = kg % 2
            xnfm = xnfm2[kgp]
            s_xnfm = s_xnfm2[kgp]
            for tt4 in range(4):
                tpar = (kg * 4 + tt4) % 3
                xt, sxt = x_t[tpar], s_xt[tpar]
                xn, sxn = xn_tm[(kg * 4 + tt4) % 2], s_xntm[(kg * 4 + tt4) % 2]
                row0 = kg * 512 + tt4 * 128

                ev = st("Q", lambda q, xt=xt, row0=row0:
                        q.dma_start(xt, xin[row0:row0 + 128, :]),
                        waits=prod_waits(sxt), dma="dQ")
                sxt.produced(ev)

                ssum = st_s[:, 0:1]
                nmean = st_s[:, 1:2]
                ssq = st_s[:, 2:3]
                var = st_s[:, 3:4]
                sd = st_s[:, 4:5]
                rstd = st_s[:, 5:6]
                t1 = st_s[:, 6:7]
                ev1 = st("V", lambda v, xt=xt: v.reduce_sum(ssum, xt, axis=AX),
                         waits=cons_waits(sxt) + prod_waits(s_stats))
                ev2 = st("S", lambda s, xt=xt: s.activation(
                    sqs, xt, ACT.Square, accum_out=ssq),
                    waits=cons_waits(sxt) + [ev1])
                sxt.used(ev2)
                ev3 = st("V", lambda v: v.tensor_scalar(
                    nmean, ssum, -1.0 / DM, None, op0=ALU.mult), waits=[ev1])
                ev4 = st("V", lambda v: v.tensor_scalar(
                    var, ssq, 1.0 / DM, None, op0=ALU.mult), waits=[ev2])
                ev5 = st("G", lambda g: g.tensor_mul(
                    t1, nmean, nmean), waits=[ev3])
                ev6 = st("G", lambda g: g.tensor_sub(var, var, t1),
                         waits=[ev4, ev5])
                ev7 = st("S", lambda s: s.activation(
                    sd, var, ACT.Sqrt, bias=eps_s, scale=1.0), waits=[ev6])
                ev8 = st("V", lambda v: v.reciprocal(rstd, sd), waits=[ev7])
                nmr = st_s[:, 7:8]
                ev8b = st("G", lambda g: g.tensor_mul(nmr, nmean, rstd),
                          waits=[ev8])
                ev9 = st("S", lambda s, xt=xt, xn=xn: s.activation(
                    xn, xt, ACT.Identity, bias=nmr, scale=rstd),
                    waits=prod_waits(sxn) + [ev8b])
                sxt.used(ev9)
                sxn.produced(ev9)
                s_stats.produced(ev8)
                s_stats.used(ev9)

                if tt4 == 0:
                    xnfm_evs = []
                ev = st("Q", lambda q, xn=xn, tt4=tt4, xnfm=xnfm: q.dma_start(
                    xnfm[:, :, tt4 * 128:(tt4 + 1) * 128], xn, transpose=True),
                    waits=cons_waits(sxn) + (prod_waits(s_xnfm)
                                             if tt4 == 0 else []),
                    dma="dQ", xpose=True)
                sxn.used(ev)
                xnfm_evs.append(ev)
                yield
            s_xnfm.produced(xnfm_evs)
            ev_xnfm = xnfm_evs

            for half in range(2):
                def fn(q, half=half, kg=kg, xnfm=xnfm):
                    dap = bass.AP(xnf_scr.tensor,
                                  half * 8 * 128 * SEQ + kg * 512,
                                  [[SEQ, 128], [128 * SEQ, 8], [1, 512]])
                    return q.dma_start(dap, xnfm[:, half * 8:half * 8 + 8, :])
                ev = st("Q", fn, waits=list(ev_xnfm), dma="dQ")
                s_xnfm.used(ev)
                scr_slot["xnf"].produced(ev)

        def back(kg):
            kgp = kg % 2
            xnfm = xnfm2[kgp]
            s_xnfm = s_xnfm2[kgp]
            kvl = kvl2[kgp]
            s_kvl = s_kvl2[kgp]
            ev = st("S", lambda s, kg=kg, kgp=kgp: s.dma_start(
                cslc[kgp], ck_scr[:, kg * 512:(kg + 1) * 512]),
                waits=prod_waits(s_cslc[kgp]) +
                cons_waits(tbl_slots["ck_scr"]), dma="dA")
            s_cslc[kgp].produced(ev)
            ev = st("S", lambda s, kg=kg, kgp=kgp: s.dma_start(
                sslc[kgp], sk_scr[:, kg * 512:(kg + 1) * 512]),
                waits=prod_waits(s_sslc[kgp]) +
                cons_waits(tbl_slots["sk_scr"]), dma="dA")
            s_sslc[kgp].produced(ev)
            for lc in range(4):
                bank = lc % 2
                pss, psa = ps_slot[bank], ps_ap[bank]

                def fn(t, lc=lc, psa=psa, xnfm=xnfm):
                    r = None
                    for fc in range(NH):
                        r = t.matmul(psa, wdn_s[:, fc, lc * 128:(lc + 1) * 128],
                                     xnfm[:, fc, :],
                                     start=(fc == 0), stop=(fc == NH - 1))
                    return r
                ev = st("T", fn, waits=prod_waits(pss) +
                        cons_waits(s_xnfm, s_wdn))
                pss.produced(ev)
                s_xnfm.used(ev)
                s_wdn.used(ev)
                evv = st("V", lambda v, lc=lc, psa=psa, kvl=kvl: v.tensor_copy(
                    kvl[:, lc, :], psa),
                    waits=cons_waits(pss) + (prod_waits(s_kvl)
                                             if lc == 0 else []))
                pss.used(evv)
                if lc == 3:
                    s_kvl.produced(evv)
                elif lc == 0:
                    s_kvl.users = []
            yield

            for which in ("k", "v"):
                wsl = s_wku if which == "k" else s_wvu
                wts = wku_s if which == "k" else wvu_s
                stg = kst if which == "k" else vst
                sstg = s_kst if which == "k" else s_vst
                for fc in range(NH):
                    if fc == 8:
                        yield
                    bank = 2 + (fc % 2)
                    pss, psa = ps_slot[bank], ps_ap[bank]

                    def fn(t, fc=fc, wts=wts, psa=psa, kvl=kvl):
                        r = None
                        for lc in range(4):
                            r = t.matmul(psa,
                                         wts[:, lc, fc * 128:(fc + 1) * 128],
                                         kvl[:, lc, :],
                                         start=(lc == 0), stop=(lc == 3))
                        return r
                    ev = st("T", fn, waits=prod_waits(pss) +
                            cons_waits(s_kvl, wsl))
                    pss.produced(ev)
                    s_kvl.used(ev)
                    wsl.used(ev)

                    if which == "v":
                        evv = st("S", lambda s, fc=fc, psa=psa, stg=stg:
                                 s.copy(stg[:, fc, :], psa),
                                 waits=cons_waits(pss) +
                                 (prod_waits(sstg) if fc == 0 else []))
                        pss.used(evv)
                        if fc == NH - 1:
                            sstg.produced(evv)
                        elif fc == 0:
                            sstg.users = []
                    else:
                        rp = fc % 2
                        cslice = cslc[kgp]
                        sslice = sslc[kgp]
                        ev1 = st("V", lambda v, psa=psa, rp=rp, s=sslice: (
                            v.tensor_mul(rot[rp][0:64, :], psa[64:128, :],
                                         s[0:64, :]),
                            v.tensor_mul(rot[rp][64:128, :], psa[0:64, :],
                                         s[64:128, :]))[-1],
                            waits=cons_waits(pss, s_sslc[kgp]) +
                            prod_waits(s_rot[rp]))
                        s_rot[rp].produced(ev1)
                        s_sslc[kgp].used(ev1)
                        ev2 = st("V", lambda v, psa=psa, rp=rp, c=cslice:
                                 v.tensor_mul(m1[rp], psa, c),
                                 waits=cons_waits(pss, s_cslc[kgp]) +
                                 prod_waits(s_m1[rp]))
                        pss.used(ev1)
                        pss.used(ev2)
                        s_m1[rp].produced(ev2)
                        s_cslc[kgp].used(ev2)
                        ev4 = st("G", lambda g, fc=fc, rp=rp, stg=stg:
                                 g.tensor_add(stg[:, fc, :], m1[rp], rot[rp]),
                                 waits=cons_waits(s_m1[rp], s_rot[rp]) +
                                 (prod_waits(sstg) if fc == 0 else []))
                        s_m1[rp].used(ev4)
                        s_rot[rp].used(ev4)
                        if fc == NH - 1:
                            sstg.produced(ev4)
                        elif fc == 0:
                            sstg.users = []

                dst = k_scr if which == "k" else v_scr
                sname = which
                for half in range(2):
                    def fn(q, dst=dst, half=half, stg=stg, kg=kg):
                        dap = bass.AP(dst.tensor,
                                      half * 8 * 128 * SEQ + kg * 512,
                                      [[SEQ, 128], [128 * SEQ, 8], [1, 512]])
                        return q.dma_start(dap, stg[:, half * 8:half * 8 + 8, :])
                    ev = st("Q", fn, waits=cons_waits(sstg), dma="dQ")
                    sstg.used(ev)
                    scr_slot[sname].produced(ev)
                yield

        def run_all(g):
            for _ in g:
                pass

        run_all(front(0))
        for kg in range(NKG):
            if kg + 1 < NKG:
                run_all(front(kg + 1))
            run_all(back(kg))

        plan.barrier()

    if "B" in PHASES:
        cf = Carver(arF, ARF_U)
        rotB = [cf.take([512]) for _ in range(2)]
        m1B = [cf.take([512]) for _ in range(2)]
        qf = [cf.take([512]) for _ in range(2)]
        cslq = [cf.take([512]) for _ in range(2)]
        sslq = [cf.take([512]) for _ in range(2)]
        cb = Carver(arB, ARB_U)
        xnfmB2 = [cb.take([NH, 512]) for _ in range(2)]
        qst = cb.take([NH, 512])
        cc = Carver(ctxr, CTX_U)
        wq_s = cc.take([NH, DM])

        s_cslq = [Slot(), Slot()]
        s_sslq = [Slot(), Slot()]
        s_xnfmB2 = [Slot(), Slot()]
        s_wq = Slot()
        s_qst = Slot()
        s_rot = [Slot(), Slot()]
        s_m1 = [Slot(), Slot()]
        s_qf = [Slot(), Slot()]

        def fnw(g):
            sap = bass.AP(wq.tensor, 0,
                          [[DM, 128], [128 * DM, NH], [1, DM]])
            return g.dma_start(wq_s, sap)
        s_wq.produced(st("G", fnw, dma="dG"))

        def bload(r):
            rp = r % 2
            ev = st("S", lambda s, r=r, rp=rp: s.dma_start(
                cslq[rp], cq_scr[:, r * 512:(r + 1) * 512]),
                waits=prod_waits(s_cslq[rp]), dma="dA")
            s_cslq[rp].produced(ev)
            ev = st("S", lambda s, r=r, rp=rp: s.dma_start(
                sslq[rp], sq_scr2[:, r * 512:(r + 1) * 512]),
                waits=prod_waits(s_sslq[rp]), dma="dA")
            s_sslq[rp].produced(ev)
            evs = []
            for j in range(4):
                tt = 4 * r + j

                def fn(g, tt=tt, j=j, rp=rp):
                    with g.register(f"rb{tt}") as rg:
                        g.reg_load(rg, idx_s[0:1, 32 + tt:32 + tt + 1])
                        sap = bass.AP(xnf_scr.tensor, rg,
                                      [[SEQ, 128], [128 * SEQ, NH], [1, 128]])
                        return g.dma_start(
                            xnfmB2[rp][:, :, j * 128:(j + 1) * 128], sap)
                ev = st("G", fn, waits=(prod_waits(s_xnfmB2[rp])
                                        if j == 0 else [])
                        + cons_waits(scr_slot["xnf"]) + [ev_idx], dma="dG")
                evs.append(ev)
                scr_slot["xnf"].used(ev)
            s_xnfmB2[rp].produced(evs)

        bload(0)
        for r in range(NOG):
            rp = r % 2
            xnfmB = xnfmB2[rp]
            s_xnfm = s_xnfmB2[rp]
            if r + 1 < NOG:
                bload(r + 1)

            for oc in range(NH):
                bank = oc % 2
                pss, psa = ps_slot[bank], ps_ap[bank]

                def fn(t, oc=oc, psa=psa, xnfmB=xnfmB):
                    rr = None
                    for fc in range(NH):
                        rr = t.matmul(psa,
                                      wq_s[:, fc, oc * 128:(oc + 1) * 128],
                                      xnfmB[:, fc, :],
                                      start=(fc == 0), stop=(fc == NH - 1))
                    return rr
                ev = st("T", fn, waits=prod_waits(pss) +
                        cons_waits(s_xnfm, s_wq))
                pss.produced(ev)
                s_xnfm.used(ev)
                s_wq.used(ev)

                rp = oc % 2
                ev0 = st("S", lambda s, oc=oc, psa=psa, rp=rp: s.activation(
                    qf[rp], psa, ACT.Identity, bias=bq_s[:, oc:oc + 1],
                    scale=1.0),
                    waits=cons_waits(pss) + prod_waits(s_qf[rp]) + [ev_bias])
                pss.used(ev0)
                s_qf[rp].produced(ev0)
                cslice = cslq[r % 2]
                sslice = sslq[r % 2]
                ev1 = st("V", lambda v, rp=rp: (
                    v.tensor_copy(rotB[rp][0:64, :], qf[rp][64:128, :]),
                    v.tensor_copy(rotB[rp][64:128, :], qf[rp][0:64, :]))[-1],
                    waits=cons_waits(s_qf[rp]) + prod_waits(s_rot[rp]))
                s_rot[rp].produced(ev1)
                ev2 = st("V", lambda v, rp=rp, c=cslice:
                         v.tensor_mul(m1B[rp], qf[rp], c),
                         waits=cons_waits(s_qf[rp], s_cslq[r % 2]) +
                         prod_waits(s_m1[rp]))
                s_qf[rp].used(ev1)
                s_qf[rp].used(ev2)
                s_m1[rp].produced(ev2)
                s_cslq[r % 2].used(ev2)
                ev3 = st("G", lambda g, rp=rp, s=sslice:
                         g.tensor_mul(rotB[rp], rotB[rp], s),
                         waits=cons_waits(s_rot[rp], s_sslq[r % 2]))
                s_rot[rp].produced(ev3)
                s_sslq[r % 2].used(ev3)
                ev4 = st("V", lambda v, oc=oc, rp=rp:
                         v.tensor_add(qst[:, oc, :], m1B[rp], rotB[rp]),
                         waits=cons_waits(s_m1[rp], s_rot[rp]) +
                         (prod_waits(s_qst) if oc == 0 else []))
                s_m1[rp].used(ev4)
                s_rot[rp].used(ev4)
                if oc == NH - 1:
                    s_qst.produced(ev4)
                elif oc == 0:
                    s_qst.users = []

            for half in range(2):
                def fn(q, half=half, r=r):
                    dap = bass.AP(q_scr.tensor,
                                  half * 8 * 128 * TOK + r * 512,
                                  [[TOK, 128], [128 * TOK, 8], [1, 512]])
                    return q.dma_start(dap, qst[:, half * 8:half * 8 + 8, :])
                ev = st("Q", fn, waits=cons_waits(s_qst), dma="dQ")
                s_qst.used(ev)
                scr_slot["q"].produced(ev)

        plan.barrier()

    if "C" in PHASES:
        ccv = Carver(arB, ARB_U)
        k_s = [ccv.take([SEQ]) for _ in range(2)]
        v_t = [ccv.take([32, 128]) for _ in range(2)]
        q_s = [ccv.take([TOK]) for _ in range(2)]
        P_b = [ccv.take([SEQ]) for _ in range(3)]
        PT_b = [ccv.take([32, 128]) for _ in range(3)]
        cfv = Carver(arF, ARF_U)
        mks_s = cfv.take([NH, 256])
        ctx = ar_view(ctxr, 0, [NH, DM])

        s_k = [Slot(), Slot()]
        s_v = [Slot(), Slot()]
        s_q = [Slot(), Slot()]
        s_P = [Slot() for _ in range(3)]
        s_PT = [Slot() for _ in range(3)]
        s_m = Slot()
        s_ctx = Slot()
        s_lac = [Slot() for _ in range(4)]
        s_rl = [Slot() for _ in range(4)]

        s_m.produced(st("Q", lambda q: q.dma_start(mks_s, mks), dma="dQ"))

        def head_loads(h):
            hp = h % 2
            ev = st("S", lambda s, h=h, hp=hp: s.dma_start(
                k_s[hp], k_scr[h * 128:(h + 1) * 128, :]),
                waits=prod_waits(s_k[hp]) + cons_waits(scr_slot["k"]),
                dma="dA")
            s_k[hp].produced(ev)
            ev = st("Q", lambda s, h=h, hp=hp: s.dma_start(
                v_t[hp], v_scr[h * 128:(h + 1) * 128, :], transpose=True),
                waits=prod_waits(s_v[hp]) + cons_waits(scr_slot["v"]),
                dma="dQ", xpose=True)
            s_v[hp].produced(ev)
            ev = st("S", lambda s, h=h, hp=hp: s.dma_start(
                q_s[hp], q_scr[h * 128:(h + 1) * 128, :]),
                waits=prod_waits(s_q[hp]) + cons_waits(scr_slot["q"]),
                dma="dA")
            s_q[hp].produced(ev)

        head_loads(0)
        pend = []

        def emit_ctx(p):
            (par, hp, W, tt, h) = p
            cbank = 4 + (len(pend) + h + tt) % 2
            cps, cpa = ps_slot[cbank], ps_ap[cbank]

            def fn(t, par=par, hp=hp, W=W, cpa=cpa):
                r = None
                for kc in range(W // 128):
                    r = t.matmul(cpa[:, 0:128], PT_b[par][:, kc, :],
                                 v_t[hp][:, kc, :],
                                 start=(kc == 0), stop=(kc == W // 128 - 1))
                return r
            ev_c = st("T", fn, waits=prod_waits(cps) +
                      cons_waits(s_PT[par], s_v[hp]))
            cps.produced(ev_c)
            s_PT[par].used(ev_c)
            s_v[hp].used(ev_c)

            ev_e = st("V", lambda v, tt=tt, h=h, par=par, cpa=cpa:
                      v.tensor_scalar(ctx[:, tt, h * 128:(h + 1) * 128],
                                      cpa[:, 0:128],
                                      rl_s[:, par:par + 1], None,
                                      op0=ALU.mult),
                      waits=cons_waits(cps, s_rl[par]))
            cps.used(ev_e)
            s_rl[par].used(ev_e)
            s_ctx.produced(ev_e)

        pancnt = 0
        for h in range(NH):
            hp = h % 2

            for tt in range(16):
                if tt == 5 and h + 1 < NH:
                    head_loads(h + 1)
                par = pancnt % 3
                pancnt += 1
                W = wpad(tt)
                nch = (W + 511) // 512
                exp_evs = []
                ev_nc = None
                for ci in range(nch):
                    n0 = ci * 512
                    N = min(512, W - n0)
                    bank = (ci + pancnt * 4) % 4
                    pss, psa = ps_slot[bank], ps_ap[bank]
                    ev_mm = st("T", lambda t, tt=tt, hp=hp, n0=n0, N=N, psa=psa:
                               t.matmul(psa[:, 0:N],
                                        q_s[hp][:, tt * 128:(tt + 1) * 128],
                                        k_s[hp][:, n0:n0 + N],
                                        start=True, stop=True),
                               waits=prod_waits(pss) +
                               cons_waits(s_q[hp], s_k[hp]))
                    pss.produced(ev_mm)
                    s_q[hp].used(ev_mm)
                    s_k[hp].used(ev_mm)
                    ev_ready = ev_mm
                    if ci == nch - 1:
                        moff = (W - 256) - n0
                        assert moff >= 0
                        ev_ready = st("V", lambda v, psa=psa, moff=moff, tt=tt:
                                      v.tensor_add(psa[:, moff:moff + 256],
                                                   psa[:, moff:moff + 256],
                                                   mks_s[:, tt, :]),
                                      waits=[ev_mm] + cons_waits(s_m))
                        pss.produced(ev_ready)
                    if ci == 0:
                        N0 = N
                        ev_nc = st("V", lambda v, psa=psa, N0=N0, par=par:
                                   v.reduce_max(negc[:, par:par + 1],
                                                psa[:, 0:N0],
                                                axis=AX, negate=True),
                                   waits=[ev_ready] + prod_waits(s_lac[par]))
                        pss.used(ev_nc)
                    ev = st("S", lambda s, psa=psa, n0=n0, N=N, par=par, ci=ci:
                            s.activation(P_b[par][:, n0:n0 + N], psa[:, 0:N],
                                         ACT.Exp, bias=negc[:, par:par + 1],
                                         scale=1.0,
                                         accum_out=lacc[:, par * 8 + ci:
                                                        par * 8 + ci + 1]),
                            waits=[ev_ready, ev_nc] +
                            (prod_waits(s_P[par]) if ci == 0 else []))
                    pss.used(ev)
                    exp_evs.append(ev)
                s_P[par].produced(exp_evs[-1])
                s_lac[par].produced(exp_evs[-1])

                ev_l = st("V", lambda v, par=par, nch=nch: v.reduce_sum(
                    lsum[:, par:par + 1], lacc[:, par * 8:par * 8 + nch],
                    axis=AX),
                    waits=exp_evs)
                ev_rl = st("V", lambda v, par=par: v.reciprocal(
                    rl_s[:, par:par + 1], lsum[:, par:par + 1]),
                    waits=[ev_l] + prod_waits(s_rl[par]))
                s_rl[par].produced(ev_rl)
                s_lac[par].used(ev_l)

                ev_t = st("Q", lambda q, par=par, W=W: q.dma_start(
                    PT_b[par][:, 0:W // 128, :], P_b[par][:, 0:W],
                    transpose=True),
                    waits=cons_waits(s_P[par]) + prod_waits(s_PT[par]),
                    dma="dQ", xpose=True)
                s_P[par].used(ev_t)
                s_PT[par].produced(ev_t)

                pend.append((par, hp, W, tt, h))
                if len(pend) > 2:
                    emit_ctx(pend.pop(0))

        while pend:
            emit_ctx(pend.pop(0))

        plan.barrier()

    if "D" in PHASES:
        cd = Carver(arB, ARB_U)
        ctxfm2 = [cd.take([NH, 512]), ar_view(ctxr, 0, [NH, 512])]
        wo_all = cd.take([NH, DM])
        x1nt = cd.take([DM])
        cfd = Carver(arF, ARF_U)
        att2 = [cfd.take([DM]) for _ in range(2)]
        xtd = [cfd.take([DM]) for _ in range(2)]
        tmpd = cfd.take([DM])
        bo_s = cfd.take([DM])

        s_cfm2 = [Slot(), Slot()]
        s_wo = Slot()
        s_att2 = [Slot(), Slot()]
        s_xt = [Slot(), Slot()]
        s_x1 = Slot()
        s_tmp = Slot()
        s_stats = Slot()
        s_bo = Slot()

        s_bo.produced(st("Q", lambda q: q.dma_start(bo_s, bob), dma="dQ"))
        woevs = []
        for og in range(4):
            def fnw(s, og=og):
                sap = bass.AP(wo_scr.tensor, og * 512,
                              [[DM, 128], [128 * DM, NH], [1, 512]])
                return s.dma_start(wo_all[:, :, og * 512:(og + 1) * 512], sap)
            woevs.append(st("S", fnw, dma="dA"))
        s_wo.produced(woevs)

        def cfm_load(ttb):
            bp = ttb % 2
            tevs = []
            for j in range(4):
                tt = ttb * 4 + j
                ev = st("Q", lambda q, tt=tt, j=j, bp=bp: q.dma_start(
                    ctxfm2[bp][:, :, j * 128:(j + 1) * 128],
                    ar_view(ctxr, 0, [NH, DM])[:, tt, :],
                    transpose=True),
                    waits=cons_waits(s_ctx) +
                    (prod_waits(s_cfm2[bp]) if j == 0 else []), dma="dQ",
                    xpose=True)
                tevs.append(ev)
                s_ctx.used(ev)
            s_cfm2[bp].produced(tevs)

        cfm_load(0)
        s_cfm2[1].users = list(s_cfm2[0].ready)
        cfm_load(1)
        for tt in range(16):
            ttb = tt // 4
            ctxfm = ctxfm2[ttb % 2]
            s_cfm = s_cfm2[ttb % 2]
            tp = tt % 2
            att = att2[tp]
            s_att = s_att2[tp]
            j4 = tt % 4
            for og in range(4):
                bank = og
                pss, psa = ps_slot[bank], ps_ap[bank]

                def fn(t, j4=j4, og=og, psa=psa, ctxfm=ctxfm):
                    r = None
                    for fc in range(NH):
                        r = t.matmul(psa,
                                     ctxfm[:, fc, j4 * 128:(j4 + 1) * 128],
                                     wo_all[:, fc, og * 512:(og + 1) * 512],
                                     start=(fc == 0), stop=(fc == NH - 1))
                    return r
                ev = st("T", fn, waits=prod_waits(pss) +
                        cons_waits(s_cfm, s_wo))
                pss.produced(ev)
                s_cfm.used(ev)
                s_wo.used(ev)

                ev = st("V", lambda v, og=og, psa=psa, att=att:
                        v.tensor_add(att[:, og * 512:(og + 1) * 512],
                                     psa, bo_s[:, og * 512:(og + 1) * 512]),
                        waits=cons_waits(pss, s_bo) +
                        (prod_waits(s_att) if og == 0 else []))
                pss.used(ev)
                if og == 3:
                    s_att.produced(ev)
                elif og == 0:
                    s_att.users = []
            if j4 == 3 and ttb + 2 < 4:
                cfm_load(ttb + 2)

            xp = tt % 2

            def fnx(q, tt=tt, xp=xp):
                with q.register(f"rx{tt}") as r:
                    q.reg_load(r, idx_s[0:1, 16 + tt:16 + tt + 1])
                    src = bass.AP(xin.tensor, r, [[DM, 128], [1, DM]])
                    return q.dma_start(xtd[xp], src)
            ev = st("Q", fnx, waits=prod_waits(s_xt[xp]) + [ev_idx],
                    dma="dQ")
            s_xt[xp].produced(ev)

            def ln_soup(src_ap, dst_ap, sL, sD, extra):
                ssum = st_s[:, 0:1]
                nmean = st_s[:, 1:2]
                ssq = st_s[:, 2:3]
                var = st_s[:, 3:4]
                sdv = st_s[:, 4:5]
                rstd = st_s[:, 5:6]
                t1 = st_s[:, 6:7]
                nmr = st_s[:, 7:8]
                e1 = st("V", lambda v: v.reduce_sum(ssum, src_ap, axis=AX),
                        waits=cons_waits(sL) + prod_waits(s_stats) + extra)
                e2 = st("S", lambda s: s.activation(
                    sqs, src_ap, ACT.Square, accum_out=ssq),
                    waits=cons_waits(sL) + [e1] + extra)
                e3 = st("V", lambda v: v.tensor_scalar(
                    nmean, ssum, -1.0 / DM, None, op0=ALU.mult),
                    waits=[e1])
                e4 = st("V", lambda v: v.tensor_scalar(
                    var, ssq, 1.0 / DM, None, op0=ALU.mult), waits=[e2])
                e5 = st("G", lambda g: g.tensor_mul(
                    t1, nmean, nmean), waits=[e3])
                e6 = st("G", lambda g: g.tensor_sub(var, var, t1),
                        waits=[e4, e5])
                e7 = st("S", lambda s: s.activation(
                    sdv, var, ACT.Sqrt, bias=eps_s, scale=1.0), waits=[e6])
                e8 = st("V", lambda v: v.reciprocal(rstd, sdv), waits=[e7])
                e8b = st("G", lambda g: g.tensor_mul(nmr, nmean, rstd),
                         waits=[e8])
                e9 = st("S", lambda s, src_ap=src_ap, dst_ap=dst_ap:
                        s.activation(dst_ap, src_ap, ACT.Identity,
                                     bias=nmr, scale=rstd),
                        waits=[e8b] + prod_waits(sD))
                sL.used(e9)
                sD.produced(e9)
                s_stats.produced(e9)
                s_stats.used(e9)
                return e9

            eln2 = ln_soup(att, tmpd, s_att, s_tmp, [])
            ev_x1 = st("G", lambda g, xp=xp: g.tensor_add(
                tmpd, tmpd, xtd[xp]),
                waits=cons_waits(s_tmp, s_xt[xp]) + prod_waits(s_x1))
            s_tmp.used(ev_x1)
            s_xt[xp].used(ev_x1)
            s_x1.produced(ev_x1)
            ev_w1 = st("S", lambda s, tt=tt: s.dma_start(
                x1_scr[tt * 128:(tt + 1) * 128, :], tmpd),
                waits=cons_waits(s_x1), dma="dA")
            s_x1.used(ev_w1)
            scr_slot["x1"].produced(ev_w1)
            eln3 = ln_soup(tmpd, x1nt, s_x1, s_tmp, [ev_w1])
            ev_w2 = st("S", lambda s, tt=tt: s.dma_start(
                x1n_scr[tt * 128:(tt + 1) * 128, :], x1nt),
                waits=cons_waits(s_tmp), dma="dA")
            s_tmp.used(ev_w2)
            scr_slot["x1n"].produced(ev_w2)

        plan.barrier()

    if "E" in PHASES:
        ce = Carver(arB, ARB_U)
        wg_s = [ce.take([NH, 256]) for _ in range(2)]
        wu_s = [ce.take([NH, 256]) for _ in range(2)]
        wd_s = [ce.take([8, 512]) for _ in range(2)]
        x1nfm2 = [ce.take([NH, 512]) for _ in range(2)]
        g_sb = ce.take([512])
        u_sb = ce.take([512])
        cc = Carver(ctxr, CTX_U)
        f_b = cc.take([64, 512])
        cfe = Carver(arF, ARF_U)
        dout = cfe.take([4, DM])
        x1te = cfe.take([DM])
        sd_sl = [cfe.take([512]) for _ in range(2)]

        s_wg = [Slot(), Slot()]
        s_wu = [Slot(), Slot()]
        s_fA, s_fB = Slot(), Slot()
        s_wd = [Slot(), Slot()]
        s_gs, s_us = Slot(), Slot()
        s_xnf2 = [Slot(), Slot()]
        s_do = Slot()
        s_x1t = Slot()
        s_tmp = Slot()
        s_ot = Slot()
        s_stats = Slot()
        s_sdb = [Slot(), Slot()]

        def x1nfm_load(r):
            rp = r % 2
            tevs = []
            for j in range(4):
                tt = r * 4 + j
                ev = st("Q", lambda q, tt=tt, j=j, rp=rp: q.dma_start(
                    x1nfm2[rp][:, :, j * 128:(j + 1) * 128],
                    x1n_scr[tt * 128:(tt + 1) * 128, :], transpose=True),
                    waits=cons_waits(scr_slot["x1n"]) +
                    (prod_waits(s_xnf2[rp]) if j == 0 else []), dma="dQ",
                    xpose=True)
                tevs.append(ev)
                scr_slot["x1n"].used(ev)
            s_xnf2[rp].produced(tevs)

        x1nfm_load(0)
        for r in range(NOG):
            rp = r % 2
            x1nfm = x1nfm2[rp]
            s_xnf = s_xnf2[rp]
            if r + 1 < NOG:
                x1nfm_load(r + 1)

            for ic in range(64):
                sl = ic // 2
                slp = sl % 2
                icw = ic % 2
                if icw == 0:
                    for wsl, wptr, w8 in ((s_wg[slp], wg_s[slp], wg8),
                                          (s_wu[slp], wu_s[slp], wu8)):
                        def fnw(g, wptr=wptr, w8=w8, sl=sl):
                            sap = bass.AP(w8.tensor, sl * 256,
                                          [[ITER, 128], [128 * ITER, NH],
                                           [1, 256]])
                            return g.dma_start(wptr, sap)
                        ev = st("G", fnw, waits=prod_waits(wsl), dma="dG")
                        wsl.produced(ev)

                gb, ub = ps_slot[ic % 2], ps_slot[2 + ic % 2]
                ga, ua = ps_ap[ic % 2], ps_ap[2 + ic % 2]

                def fng(t, icw=icw, slp=slp, ga=ga, x1nfm=x1nfm):
                    rr = None
                    for fc in range(NH):
                        rr = t.matmul(ga, wg_s[slp][:, fc,
                                                    icw * 128:(icw + 1) * 128],
                                      x1nfm[:, fc, :],
                                      start=(fc == 0), stop=(fc == NH - 1))
                    return rr
                evg = st("T", fng, waits=prod_waits(gb) +
                         cons_waits(s_wg[slp], s_xnf))
                gb.produced(evg)
                s_wg[slp].used(evg)
                s_xnf.used(evg)

                def fnu(t, icw=icw, slp=slp, ua=ua, x1nfm=x1nfm):
                    rr = None
                    for fc in range(NH):
                        rr = t.matmul(ua, wu_s[slp][:, fc,
                                                    icw * 128:(icw + 1) * 128],
                                      x1nfm[:, fc, :],
                                      start=(fc == 0), stop=(fc == NH - 1))
                    return rr
                evu = st("T", fnu, waits=prod_waits(ub) +
                         cons_waits(s_wu[slp], s_xnf))
                ub.produced(evu)
                s_wu[slp].used(evu)
                s_xnf.used(evu)

                ev1 = st("S", lambda s, ic=ic, ga=ga: s.activation(
                    g_sb, ga, ACT.Silu, bias=0.0, scale=sg_s[:, ic:ic + 1]),
                    waits=cons_waits(gb) + prod_waits(s_gs) + [ev_bias])
                gb.used(ev1)
                s_gs.produced(ev1)
                ev2 = st("V", lambda v, ic=ic, ua=ua: v.tensor_scalar(
                    u_sb, ua, su_s[:, ic:ic + 1], None, op0=ALU.mult),
                    waits=cons_waits(ub) + prod_waits(s_us) + [ev_bias])
                ub.used(ev2)
                s_us.produced(ev2)
                s_f = s_fA if ic < 32 else s_fB
                ev3 = st("V", lambda v, ic=ic: v.tensor_mul(
                    f_b[:, ic, :], g_sb, u_sb),
                    waits=cons_waits(s_gs, s_us) +
                    (prod_waits(s_f) if ic in (0, 32) else []))
                s_gs.used(ev3)
                s_us.used(ev3)
                if ic == 31:
                    s_fA.produced(ev3)
                elif ic == 63:
                    s_fB.produced(ev3)
                elif ic in (0, 32):
                    s_f.users = []

            for og in range(4):
                ogp = og % 2
                ev = st("S", lambda s, og=og, ogp=ogp: s.dma_start(
                    sd_sl[ogp], sdb[:, og * 512:(og + 1) * 512]),
                    waits=prod_waits(s_sdb[ogp]), dma="dA")
                s_sdb[ogp].produced(ev)
                for quar in range(8):
                    qp = quar % 2
                    def fnw(g, og=og, quar=quar, qp=qp):
                        sap = bass.AP(wd8.tensor,
                                      quar * 8 * 128 * DM + og * 512,
                                      [[DM, 128], [128 * DM, 8], [1, 512]])
                        return g.dma_start(wd_s[qp], sap)
                    ev = st("G", fnw, waits=prod_waits(s_wd[qp]), dma="dG")
                    s_wd[qp].produced(ev)

                    for j in range(4):
                        bank = 4 + j
                        pss, psa = ps_slot[bank], ps_ap[bank]

                        def fn(t, j=j, quar=quar, qp=qp, psa=psa):
                            rr = None
                            for ic in range(8):
                                rr = t.matmul(
                                    psa, f_b[:, quar * 8 + ic,
                                             j * 128:(j + 1) * 128],
                                    wd_s[qp][:, ic, :],
                                    start=(quar == 0 and ic == 0),
                                    stop=(quar == 7 and ic == 7))
                            return rr
                        s_f = s_fA if quar < 4 else s_fB
                        ev = st("T", fn, waits=(prod_waits(pss)
                                                if quar == 0 else []) +
                                cons_waits(s_f, s_wd[qp]))
                        if quar == 0:
                            pss.produced(ev)
                        else:
                            pss.ready = [ev]
                        s_f.used(ev)
                        s_wd[qp].used(ev)

                        if quar == 7:
                            ev = st("V", lambda v, j=j, og=og, psa=psa,
                                    ogp=ogp:
                                    v.tensor_mul(
                                        dout[:, j, og * 512:(og + 1) * 512],
                                        psa, sd_sl[ogp]),
                                    waits=cons_waits(pss, s_sdb[ogp]) +
                                    (prod_waits(s_do)
                                     if og == 0 and j == 0 else []))
                            pss.used(ev)
                            s_sdb[ogp].used(ev)
                            if og == 3 and j == 3:
                                s_do.produced(ev)
                            elif og == 0 and j == 0:
                                s_do.users = []

            for j in range(4):
                tt = r * 4 + j
                dtile = dout[:, j, :]
                ev_x1 = st("S", lambda s, tt=tt: s.dma_start(
                    x1te, x1_scr[tt * 128:(tt + 1) * 128, :]),
                    waits=prod_waits(s_x1t) + cons_waits(scr_slot["x1"]),
                    dma="dA")
                s_x1t.produced(ev_x1)

                ssum = st_s[:, 0:1]
                nmean = st_s[:, 1:2]
                ssq = st_s[:, 2:3]
                var = st_s[:, 3:4]
                sdv = st_s[:, 4:5]
                rstd = st_s[:, 5:6]
                t1 = st_s[:, 6:7]
                e1 = st("V", lambda v, dtile=dtile: v.reduce_sum(
                    ssum, dtile, axis=AX),
                    waits=cons_waits(s_do) + prod_waits(s_stats))
                e2 = st("S", lambda s, dtile=dtile: s.activation(
                    sqs, dtile, ACT.Square, accum_out=ssq),
                    waits=cons_waits(s_do) + [e1])
                e3 = st("V", lambda v: v.tensor_scalar(
                    nmean, ssum, -1.0 / DM, None, op0=ALU.mult), waits=[e1])
                e4 = st("V", lambda v: v.tensor_scalar(
                    var, ssq, 1.0 / DM, None, op0=ALU.mult), waits=[e2])
                e5 = st("G", lambda g: g.tensor_mul(
                    t1, nmean, nmean), waits=[e3])
                e6 = st("G", lambda g: g.tensor_sub(var, var, t1),
                        waits=[e4, e5])
                e7 = st("S", lambda s: s.activation(
                    sdv, var, ACT.Sqrt, bias=eps_s, scale=1.0), waits=[e6])
                e8 = st("V", lambda v: v.reciprocal(rstd, sdv), waits=[e7])
                e9 = st("V", lambda v, dtile=dtile: v.tensor_scalar(
                    dtile, dtile, nmean, rstd, op0=ALU.add, op1=ALU.mult),
                    waits=[e8] + prod_waits(s_tmp))
                s_do.used(e9)
                s_tmp.produced(e9)
                s_stats.produced(e8)
                s_stats.used(e9)
                ev_o = st("V", lambda v, dtile=dtile: v.tensor_add(
                    dtile, dtile, x1te),
                    waits=cons_waits(s_tmp, s_x1t) + [e9])
                s_tmp.used(ev_o)
                s_x1t.used(ev_o)
                s_do.used(ev_o)
                s_ot.produced(ev_o)
                ev_w = st("S", lambda s, tt=tt, dtile=dtile: s.dma_start(
                    out[tt * 128:(tt + 1) * 128, :], dtile),
                    waits=cons_waits(s_ot), dma="dA")
                s_ot.used(ev_w)
                s_do.used(ev_w)

        plan.barrier()

    with nc.Block() as block:
        block.tensor(lambda t: plan.replay(t, "T", sems))
        block.vector(lambda v: plan.replay(v, "V", sems))
        block.scalar(lambda s: plan.replay(s, "S", sems))
        block.gpsimd(lambda g: plan.replay(g, "G", sems))
        block.sync(lambda q: plan.replay(q, "Q", sems))

    stack.close()
    return nc


def _f32_trig(pos):
    import jax
    import jax.numpy as jnp
    try:
        dev = jax.local_devices(backend="cpu")[0]
    except Exception:
        dev = None
    import contextlib
    cmgr = jax.default_device(dev) if dev is not None else contextlib.nullcontext()
    with cmgr:
        inv = 1.0 / 10000.0 ** (jnp.arange(0, HD, 2, dtype=jnp.float32) / HD)
        t = jnp.asarray(pos, dtype=jnp.float32)
        fr = t[None, :] * inv[:, None]
        c = np.asarray(jnp.cos(fr)).astype(np.float64)
        s = np.asarray(jnp.sin(fr)).astype(np.float64)
    return c, s


def _tables(own_pos):
    def mk(pos, scale):
        c, s = _f32_trig(pos)
        c = c * scale
        s = s * scale
        cosf = np.concatenate([c, c], axis=0)
        sinf = np.concatenate([-s, s], axis=0)
        out = []
        for tab in (cosf, sinf):
            hi = tab.astype(np.float32)
            lo = (tab - hi.astype(np.float64)).astype(np.float32)
            out += [np.ascontiguousarray(hi), np.ascontiguousarray(lo)]
        return out

    kk = mk(np.arange(SEQ), 1.0)
    scl = 1.0 / np.float64(np.float32(np.sqrt(np.float64(HD))))
    qq = mk(own_pos, scl)
    return kk, qq


def _masks(qbtab):
    tri = np.triu(np.full((128, 128), NEG, dtype=np.float32), k=1)
    m = np.zeros((16, 128, 256), dtype=np.float32)
    for tt, qb in enumerate(qbtab):
        wp = wpad(tt)
        we = 128 * (qb + 1)
        if we == wp:
            m[tt, :, 128:256] = tri
        else:
            assert we == wp - 128
            m[tt, :, 0:128] = tri
            m[tt, :, 128:256] = NEG
    return np.ascontiguousarray(
        m.transpose(1, 0, 2).reshape(128, 16 * 256).astype(np.float32))


def kernel(L0_attn_x, L0_attn_Wq, L0_attn_bq, L0_attn_Wdown, L0_attn_Wk_up,
           L0_attn_Wv_up, L0_attn_Wo, L0_attn_bo, L0_ffn_w_gate, L0_ffn_w_up,
           L0_ffn_w_down, L0_ffn_s_gate, L0_ffn_s_up, L0_ffn_s_down):
    x = np.asarray(L0_attn_x)
    fdt = x.dtype
    asf = lambda a: np.ascontiguousarray(np.asarray(a))

    bq_l = asf(np.asarray(L0_attn_bq).reshape(NH, 128).T)
    sg_l = asf(np.asarray(L0_ffn_s_gate).reshape(64, 128).T)
    su_l = asf(np.asarray(L0_ffn_s_up).reshape(64, 128).T)
    bo_b = asf(np.broadcast_to(np.asarray(L0_attn_bo)[None, :], (128, DM)))
    sd_b = asf(np.broadcast_to(np.asarray(L0_ffn_s_down)[None, :], (128, DM)))

    in_maps = []
    reasm = []
    for c in range(8):
        b, p = c // 2, c % 2
        owns = (0, 3) if p == 0 else (1, 2)
        qbtab = [qb for qb in range(32) if qb % 4 in owns]
        own_pos = np.concatenate(
            [np.arange(qb * 128, qb * 128 + 128) for qb in qbtab])
        reasm.append((b, qbtab))
        kk, qq = _tables(own_pos)
        idx = np.zeros(64, dtype=np.int32)
        for tt in range(16):
            idx[16 + tt] = qbtab[tt] * 128 * DM
            idx[32 + tt] = qbtab[tt] * 128
        in_maps.append({
            "xin": asf(x[b]),
            "wq": asf(L0_attn_Wq), "wdn": asf(L0_attn_Wdown),
            "wku": asf(L0_attn_Wk_up), "wvu": asf(L0_attn_Wv_up),
            "wo": asf(L0_attn_Wo),
            "wg8": asf(L0_ffn_w_gate), "wu8": asf(L0_ffn_w_up),
            "wd8": asf(L0_ffn_w_down),
            "bqv": bq_l, "bob": bo_b, "sgv": sg_l, "suv": su_l, "sdb": sd_b,
            "ckh": kk[0], "ckl": kk[1], "skh": kk[2], "skl": kk[3],
            "cqh": qq[0], "cql": qq[1], "sqh": qq[2], "sql": qq[3],
            "mks": _masks(qbtab),
            "idx": idx.reshape(1, 64),
        })

    nc = build_program()
    res = run_bass_kernel_spmd(nc, in_maps, core_ids=list(range(8)))

    odt = np.asarray(res.results[0]["out"]).dtype
    outp = np.zeros((NB, SEQ, DM), dtype=odt)
    for c in range(8):
        b, qbtab = reasm[c]
        o = np.asarray(res.results[c]["out"])
        for tt, qb in enumerate(qbtab):
            outp[b, qb * 128:(qb + 1) * 128, :] = o[tt * 128:(tt + 1) * 128, :]
    return outp


# version 22 (final)
# speedup vs baseline: 1.0746x; 1.0746x over previous
import numpy as np
import ml_dtypes

import concourse.bass as bass
import concourse.mybir as mybir
from concourse.bass_utils import run_bass_kernel_spmd

F32 = mybir.dt.float32
BF16 = mybir.dt.bfloat16
I8 = mybir.dt.int8
I32 = mybir.dt.int32
ALU = mybir.AluOpType
ACT = mybir.ActivationFunctionType
AX = mybir.AxisListType.X
BDT = ml_dtypes.bfloat16

DM = 2048
NH = 16
HD = 128
LAT = 512
ITER = 8192
SEQ = 4096
NB = 4
TOK = 2048
NKG = 8
NOG = 4

DEBUG_TAPS = ()
PHASES = "ABCDE"

XPOSE_CHAIN = False
EPS = 1e-5
NEG = -10000.0


def wpad(tt):
    return 256 * (tt + 1)


class Plan:

    ENGS = ("T", "V", "S", "G", "Q")
    NPOOL = 8

    def __init__(self):
        self.ops = {e: [] for e in self.ENGS}
        self.cnt = {k: 0 for k in ("T", "V", "S", "G")}
        self.qn = {}
        for q in ("dQ", "dA", "dG"):
            for i in range(self.NPOOL):
                self.cnt[f"{q}{i}"] = 0
            self.qn[q] = 0

    def step(self, eng, fn, waits=(), dma=None, xpose=False):
        ws = []
        for w in waits:
            if w is None:
                continue
            ws.append(w)
        if dma:
            sem = f"{dma}{self.qn[dma] % self.NPOOL}"
            self.qn[dma] += 1
            if self.cnt[sem] > 0:
                ws.append((sem, self.cnt[sem]))
            inc = 16
        else:
            sem = eng
            inc = 1
        if xpose and XPOSE_CHAIN:
            if getattr(self, "last_xpose", None) is not None:
                ws.append(self.last_xpose)
        self.ops[eng].append((tuple(ws), fn, sem, inc))
        self.cnt[sem] += inc
        ev = (sem, self.cnt[sem])
        if xpose:
            self.last_xpose = ev
        return ev

    def barrier(self):
        marks = dict(self.cnt)
        evs = tuple(marks.items())
        for e in self.ENGS:
            self.ops[e].append((evs, None, None, 0))

    def replay(self, eng_handle, eng_name, sems):
        cur = {}
        for ws, fn, sem, inc in self.ops[eng_name]:
            for s, v in ws:
                if cur.get(s, 0) < v:
                    eng_handle.wait_ge(sems[s], v)
                    cur[s] = v
            if fn is None:
                continue
            inst = fn(eng_handle)
            if inc:
                inst.then_inc(sems[sem], inc)


class Slot:

    __slots__ = ("ready", "users")

    def __init__(self):
        self.ready = []
        self.users = []

    def produced(self, ev):
        self.ready = list(ev) if isinstance(ev, list) else [ev]
        self.users = []

    def used(self, ev):
        self.users.append(ev)


def prod_waits(*slots):
    ws = []
    for s in slots:
        ws.extend(s.users)
        ws.extend(s.ready)
    return ws


def cons_waits(*slots):
    ws = []
    for s in slots:
        ws.extend(s.ready)
    return ws


def build_program():
    nc = bass.Bass("TRN2", target_bir_lowering=False)

    xin = nc.dram_tensor("xin", [SEQ, DM], F32, kind="ExternalInput").ap()
    wq = nc.dram_tensor("wq", [DM, DM], F32, kind="ExternalInput").ap()
    wdn = nc.dram_tensor("wdn", [DM, LAT], F32, kind="ExternalInput").ap()
    wku = nc.dram_tensor("wku", [LAT, DM], F32, kind="ExternalInput").ap()
    wkus = nc.dram_tensor("wkus", [LAT, DM], F32, kind="ExternalInput").ap()
    wvu = nc.dram_tensor("wvu", [LAT, DM], F32, kind="ExternalInput").ap()
    wo = nc.dram_tensor("wo", [DM, DM], F32, kind="ExternalInput").ap()
    wg8 = nc.dram_tensor("wg8", [DM, ITER], I8, kind="ExternalInput").ap()
    wu8 = nc.dram_tensor("wu8", [DM, ITER], I8, kind="ExternalInput").ap()
    wd8 = nc.dram_tensor("wd8", [ITER, DM], I8, kind="ExternalInput").ap()
    bqv = nc.dram_tensor("bqv", [128, NH], F32, kind="ExternalInput").ap()
    bob = nc.dram_tensor("bob", [128, DM], F32, kind="ExternalInput").ap()
    sgv = nc.dram_tensor("sgv", [128, 64], F32, kind="ExternalInput").ap()
    suv = nc.dram_tensor("suv", [128, 64], F32, kind="ExternalInput").ap()
    sdb = nc.dram_tensor("sdb", [128, DM], F32, kind="ExternalInput").ap()
    ckh = nc.dram_tensor("ckh", [128, SEQ], F32, kind="ExternalInput").ap()
    ckl = nc.dram_tensor("ckl", [128, SEQ], F32, kind="ExternalInput").ap()
    skh = nc.dram_tensor("skh", [128, SEQ], F32, kind="ExternalInput").ap()
    skl = nc.dram_tensor("skl", [128, SEQ], F32, kind="ExternalInput").ap()
    cqh = nc.dram_tensor("cqh", [128, TOK], F32, kind="ExternalInput").ap()
    cql = nc.dram_tensor("cql", [128, TOK], F32, kind="ExternalInput").ap()
    sqh = nc.dram_tensor("sqh", [128, TOK], F32, kind="ExternalInput").ap()
    sql = nc.dram_tensor("sql", [128, TOK], F32, kind="ExternalInput").ap()
    mks = nc.dram_tensor("mks", [128, NH * 256], F32, kind="ExternalInput").ap()
    idx = nc.dram_tensor("idx", [1, 64], I32, kind="ExternalInput").ap()

    out = nc.dram_tensor("out", [TOK, DM], F32, kind="ExternalOutput").ap()

    def scratch(name, shape, dtype):
        kind = "ExternalOutput" if name in DEBUG_TAPS else "Internal"
        return nc.dram_tensor(name, shape, dtype, kind=kind).ap()

    k_scr = scratch("k_scr", [DM, SEQ], BF16)
    v_scr = scratch("v_scr", [DM, SEQ], BF16)
    q_scr = scratch("q_scr", [DM, TOK], BF16)
    xnf_scr = scratch("xnf_scr", [DM, SEQ], BF16)
    x1_scr = scratch("x1_scr", [TOK, DM], F32)
    wo_scr = scratch("wo_scr", [DM, DM], BF16)
    ck_scr = scratch("ck_scr", [128, SEQ], F32)
    sk_scr = scratch("sk_scr", [128, SEQ], F32)
    cq_scr = scratch("cq_scr", [128, TOK], F32)
    sq_scr2 = scratch("sq_scr2", [128, TOK], F32)
    x1n_scr = scratch("x1n_scr", [TOK, DM], BF16)

    plan = Plan()
    from contextlib import ExitStack
    stack = ExitStack()
    e = stack.enter_context

    ARF_U = 12800
    ARB_U = 45056
    CTX_U = 32768
    arF = e(nc.sbuf_tensor("arF", [128, ARF_U], F32))
    arB = e(nc.sbuf_tensor("arB", [128, ARB_U], BF16))
    ctxr = e(nc.sbuf_tensor("ctxr", [128, CTX_U], BF16))
    smalls = e(nc.sbuf_tensor("smalls", [128, 256], F32))
    idx_s = e(nc.sbuf_tensor("idx_s", [1, 64], I32))
    sqs = e(nc.sbuf_tensor("sqs", [128, DM], BF16)).ap()

    def ar_view(tensor, off, shape):
        n = int(np.prod(shape))
        ap = tensor.ap()[:, off:off + n]
        if len(shape) == 2:
            ap = ap.rearrange("p (a b) -> p a b", b=shape[1])
        elif len(shape) == 3:
            ap = ap.rearrange("p (a b c) -> p a b c", b=shape[1], c=shape[2])
        return ap

    class Carver:
        def __init__(self, tensor, total):
            self.t = tensor
            self.total = total
            self.off = 0

        def take(self, shape, dtype=None):
            n = int(np.prod(shape))
            assert self.off + n <= self.total, (self.off, n, self.total)
            ap = ar_view(self.t, self.off, shape)
            self.off += n
            return ap

    sm_off = [0]

    def small(shape):
        n = int(np.prod(shape))
        ap = smalls.ap()[:, sm_off[0]:sm_off[0] + n]
        if len(shape) == 2:
            ap = ap.rearrange("p (a b) -> p a b", b=shape[1])
        sm_off[0] += n
        return ap

    eps_s = small([1])
    bq_s = small([NH])
    sg_s = small([64])
    su_s = small([64])
    st_s = small([16])
    lacc = small([32])
    lsum = small([4])
    negc = small([4])
    rl_s = small([4])
    assert sm_off[0] <= 256

    ps = [e(nc.psum_tensor(f"ps{i}", [128, 512], F32)) for i in range(8)]
    ps_ap = [p.ap() for p in ps]
    ps_slot = [Slot() for _ in range(8)]

    sems = {}
    sem_names = ["T", "V", "S", "G"] + [f"{q}{i}" for q in ("dQ", "dA", "dG")
                                        for i in range(Plan.NPOOL)]
    for name in sem_names:
        sems[name] = e(nc.semaphore(name))

    scr_slot = {n: Slot() for n in ("k", "v", "q", "xnf", "x1", "x1n")}

    st = plan.step

    ev = st("V", lambda v: v.memset(eps_s, EPS))
    ev_idx = st("Q", lambda q: q.dma_start(idx_s[:], idx), dma="dQ")
    st("Q", lambda q: q.dma_start(bq_s, bqv), dma="dQ")
    st("Q", lambda q: q.dma_start(sg_s, sgv), dma="dQ")
    ev_bias = st("Q", lambda q: q.dma_start(su_s, suv), dma="dQ")

    if "A" in PHASES:
        cf = Carver(arF, ARF_U)
        x_t = [cf.take([DM]) for _ in range(3)]
        rot = [cf.take([512]) for _ in range(2)]
        m1 = [cf.take([512]) for _ in range(2)]
        cslc = [cf.take([512]) for _ in range(2)]
        sslc = [cf.take([512]) for _ in range(2)]
        tb1 = cf.take([1024])
        tb2 = cf.take([1024])
        ca = Carver(arB, ARB_U)
        wdn_s = ca.take([NH, LAT])
        wku_s = ca.take([4, DM])
        wkus_s = ca.take([4, DM])
        xnfm0 = ca.take([NH, 512])
        xnfm1 = ca.take([NH, 512])
        xnfm2 = [xnfm0, xnfm1]
        kvl2 = [ca.take([4, 512]) for _ in range(2)]
        cc = Carver(ctxr, CTX_U)
        xn_tm = [cc.take([DM]) for _ in range(2)]
        kst = cc.take([NH, 512])
        vst = cc.take([NH, 512])
        wvu_s = cc.take([4, DM])
        assert cc.off <= CTX_U

        s_cslc = [Slot(), Slot()]
        s_sslc = [Slot(), Slot()]
        s_tb1, s_tb2 = Slot(), Slot()
        s_xnfm2 = [Slot(), Slot()]
        s_kvl2 = [Slot(), Slot()]
        s_xt = [Slot(), Slot(), Slot()]
        s_xntm = [Slot(), Slot()]
        s_kst, s_vst = Slot(), Slot()
        s_rot = [Slot(), Slot()]
        s_m1 = [Slot(), Slot()]
        s_stats = Slot()
        s_wdn, s_wku, s_wvu, s_wkus = Slot(), Slot(), Slot(), Slot()

        def wload(dst_ap, src, p_stride, nchunk, width, slot):
            def fn(g, dst_ap=dst_ap, src=src):
                sap = bass.AP(src.tensor, 0,
                              [[p_stride, 128], [128 * p_stride, nchunk],
                               [1, width]])
                return g.dma_start(dst_ap, sap)
            slot.produced(st("G", fn, dma="dG"))

        wload(wdn_s, wdn, LAT, NH, LAT, s_wdn)
        wload(wku_s, wku, DM, 4, DM, s_wku)
        wload(wkus_s, wkus, DM, 4, DM, s_wkus)
        wload(wvu_s, wvu, DM, 4, DM, s_wvu)

        tbl_slots = {}

        def build_tbl(hi, lo, dst_scr, width):
            slot = Slot()
            tbl_slots[dst_scr.tensor.name] = slot
            wevs = []
            for s0 in range(0, width, 1024):
                e1 = st("Q", lambda q, hi=hi, s0=s0: q.dma_start(
                    tb1, hi[:, s0:s0 + 1024]), waits=prod_waits(s_tb1),
                    dma="dQ")
                s_tb1.produced(e1)
                e2 = st("Q", lambda q, lo=lo, s0=s0: q.dma_start(
                    tb2, lo[:, s0:s0 + 1024]), waits=prod_waits(s_tb2),
                    dma="dQ")
                s_tb2.produced(e2)
                ev = st("V", lambda v: v.tensor_add(tb1, tb1, tb2),
                        waits=[e1, e2])
                s_tb1.produced(ev)
                s_tb2.used(ev)
                ew = st("Q", lambda q, d=dst_scr, s0=s0: q.dma_start(
                    d[:, s0:s0 + 1024], tb1), waits=[ev], dma="dQ")
                s_tb1.used(ew)
                wevs.append(ew)
            slot.produced(wevs)

        build_tbl(ckh, ckl, ck_scr, SEQ)
        build_tbl(skh, skl, sk_scr, SEQ)
        build_tbl(cqh, cql, cq_scr, TOK)
        build_tbl(sqh, sql, sq_scr2, TOK)

        for og in range(4):
            def fnc(g, og=og):
                sap = bass.AP(wo.tensor, og * 512,
                              [[DM, 128], [128 * DM, NH], [1, 512]])
                return g.dma_start(xnfm1, sap)
            ev = st("G", fnc, waits=prod_waits(s_xnfm2[1]), dma="dG")
            s_xnfm2[1].produced(ev)

            def fnw(q, og=og):
                dap = bass.AP(wo_scr.tensor, og * 512,
                              [[DM, 128], [128 * DM, NH], [1, 512]])
                return q.dma_start(dap, xnfm1)
            ev = st("Q", fnw, waits=cons_waits(s_xnfm2[1]), dma="dQ")
            s_xnfm2[1].used(ev)

        def front(kg):
            kgp = kg % 2
            xnfm = xnfm2[kgp]
            s_xnfm = s_xnfm2[kgp]
            for tt4 in range(4):
                tpar = (kg * 4 + tt4) % 3
                xt, sxt = x_t[tpar], s_xt[tpar]
                xn, sxn = xn_tm[(kg * 4 + tt4) % 2], s_xntm[(kg * 4 + tt4) % 2]
                row0 = kg * 512 + tt4 * 128

                ev = st("Q", lambda q, xt=xt, row0=row0:
                        q.dma_start(xt, xin[row0:row0 + 128, :]),
                        waits=prod_waits(sxt), dma="dQ")
                sxt.produced(ev)

                ssum = st_s[:, 0:1]
                nmean = st_s[:, 1:2]
                ssq = st_s[:, 2:3]
                var = st_s[:, 3:4]
                sd = st_s[:, 4:5]
                rstd = st_s[:, 5:6]
                t1 = st_s[:, 6:7]
                ev1 = st("V", lambda v, xt=xt: v.reduce_sum(ssum, xt, axis=AX),
                         waits=cons_waits(sxt) + prod_waits(s_stats))
                ev2 = st("S", lambda s, xt=xt: s.activation(
                    sqs, xt, ACT.Square, accum_out=ssq),
                    waits=cons_waits(sxt) + [ev1])
                sxt.used(ev2)
                ev3 = st("V", lambda v: v.tensor_scalar(
                    nmean, ssum, -1.0 / DM, None, op0=ALU.mult), waits=[ev1])
                ev4 = st("V", lambda v: v.tensor_scalar(
                    var, ssq, 1.0 / DM, None, op0=ALU.mult), waits=[ev2])
                ev5 = st("G", lambda g: g.tensor_mul(
                    t1, nmean, nmean), waits=[ev3])
                ev6 = st("G", lambda g: g.tensor_sub(var, var, t1),
                         waits=[ev4, ev5])
                ev7 = st("S", lambda s: s.activation(
                    sd, var, ACT.Sqrt, bias=eps_s, scale=1.0), waits=[ev6])
                ev8 = st("V", lambda v: v.reciprocal(rstd, sd), waits=[ev7])
                nmr = st_s[:, 7:8]
                ev8b = st("G", lambda g: g.tensor_mul(nmr, nmean, rstd),
                          waits=[ev8])
                ev9 = st("S", lambda s, xt=xt, xn=xn: s.activation(
                    xn, xt, ACT.Identity, bias=nmr, scale=rstd),
                    waits=prod_waits(sxn) + [ev8b])
                sxt.used(ev9)
                sxn.produced(ev9)
                s_stats.produced(ev8)
                s_stats.used(ev9)

                if tt4 == 0:
                    xnfm_evs = []
                ev = st("Q", lambda q, xn=xn, tt4=tt4, xnfm=xnfm: q.dma_start(
                    xnfm[:, :, tt4 * 128:(tt4 + 1) * 128], xn, transpose=True),
                    waits=cons_waits(sxn) + (prod_waits(s_xnfm)
                                             if tt4 == 0 else []),
                    dma="dQ", xpose=True)
                sxn.used(ev)
                xnfm_evs.append(ev)
                yield
            s_xnfm.produced(xnfm_evs)
            ev_xnfm = xnfm_evs

            for half in range(2):
                def fn(q, half=half, kg=kg, xnfm=xnfm):
                    dap = bass.AP(xnf_scr.tensor,
                                  half * 8 * 128 * SEQ + kg * 512,
                                  [[SEQ, 128], [128 * SEQ, 8], [1, 512]])
                    return q.dma_start(dap, xnfm[:, half * 8:half * 8 + 8, :])
                ev = st("Q", fn, waits=list(ev_xnfm), dma="dQ")
                s_xnfm.used(ev)
                scr_slot["xnf"].produced(ev)

        def back(kg):
            kgp = kg % 2
            xnfm = xnfm2[kgp]
            s_xnfm = s_xnfm2[kgp]
            kvl = kvl2[kgp]
            s_kvl = s_kvl2[kgp]
            ev = st("S", lambda s, kg=kg, kgp=kgp: s.dma_start(
                cslc[kgp], ck_scr[:, kg * 512:(kg + 1) * 512]),
                waits=prod_waits(s_cslc[kgp]) +
                cons_waits(tbl_slots["ck_scr"]), dma="dA")
            s_cslc[kgp].produced(ev)
            ev = st("S", lambda s, kg=kg, kgp=kgp: s.dma_start(
                sslc[kgp], sk_scr[:, kg * 512:(kg + 1) * 512]),
                waits=prod_waits(s_sslc[kgp]) +
                cons_waits(tbl_slots["sk_scr"]), dma="dA")
            s_sslc[kgp].produced(ev)
            for lc in range(4):
                bank = lc % 2
                pss, psa = ps_slot[bank], ps_ap[bank]

                def fn(t, lc=lc, psa=psa, xnfm=xnfm):
                    r = None
                    for fc in range(NH):
                        r = t.matmul(psa, wdn_s[:, fc, lc * 128:(lc + 1) * 128],
                                     xnfm[:, fc, :],
                                     start=(fc == 0), stop=(fc == NH - 1))
                    return r
                ev = st("T", fn, waits=prod_waits(pss) +
                        cons_waits(s_xnfm, s_wdn))
                pss.produced(ev)
                s_xnfm.used(ev)
                s_wdn.used(ev)
                evv = st("V", lambda v, lc=lc, psa=psa, kvl=kvl: v.tensor_copy(
                    kvl[:, lc, :], psa),
                    waits=cons_waits(pss) + (prod_waits(s_kvl)
                                             if lc == 0 else []))
                pss.used(evv)
                if lc == 3:
                    s_kvl.produced(evv)
                elif lc == 0:
                    s_kvl.users = []
            yield

            for which in ("k", "v"):
                wsl = s_wku if which == "k" else s_wvu
                wts = wku_s if which == "k" else wvu_s
                stg = kst if which == "k" else vst
                sstg = s_kst if which == "k" else s_vst
                for fc in range(NH):
                    if fc == 8:
                        yield
                    bank = 2 + (fc % 2)
                    pss, psa = ps_slot[bank], ps_ap[bank]

                    def fn(t, fc=fc, wts=wts, psa=psa, kvl=kvl):
                        r = None
                        for lc in range(4):
                            r = t.matmul(psa,
                                         wts[:, lc, fc * 128:(fc + 1) * 128],
                                         kvl[:, lc, :],
                                         start=(lc == 0), stop=(lc == 3))
                        return r
                    ev = st("T", fn, waits=prod_waits(pss) +
                            cons_waits(s_kvl, wsl))
                    pss.produced(ev)
                    s_kvl.used(ev)
                    wsl.used(ev)

                    if which == "v":
                        evv = st("S", lambda s, fc=fc, psa=psa, stg=stg:
                                 s.copy(stg[:, fc, :], psa),
                                 waits=cons_waits(pss) +
                                 (prod_waits(sstg) if fc == 0 else []))
                        pss.used(evv)
                        if fc == NH - 1:
                            sstg.produced(evv)
                        elif fc == 0:
                            sstg.users = []
                    else:
                        rp = fc % 2
                        bank2 = 4 + rp
                        ps2, psa2 = ps_slot[bank2], ps_ap[bank2]

                        def fn2(t, fc=fc, psa2=psa2, kvl=kvl):
                            r = None
                            for lc in range(4):
                                r = t.matmul(
                                    psa2,
                                    wkus_s[:, lc, fc * 128:(fc + 1) * 128],
                                    kvl[:, lc, :],
                                    start=(lc == 0), stop=(lc == 3))
                            return r
                        ev2m = st("T", fn2, waits=prod_waits(ps2) +
                                  cons_waits(s_kvl, s_wkus))
                        ps2.produced(ev2m)
                        s_kvl.used(ev2m)
                        s_wkus.used(ev2m)

                        ev1 = st("V", lambda v, psa2=psa2, rp=rp:
                                 v.tensor_mul(rot[rp], psa2, sslc[kgp]),
                                 waits=cons_waits(ps2, s_sslc[kgp]) +
                                 prod_waits(s_rot[rp]))
                        ps2.used(ev1)
                        s_rot[rp].produced(ev1)
                        s_sslc[kgp].used(ev1)
                        ev2 = st("V", lambda v, psa=psa, rp=rp:
                                 v.tensor_mul(m1[rp], psa, cslc[kgp]),
                                 waits=cons_waits(pss, s_cslc[kgp]) +
                                 prod_waits(s_m1[rp]))
                        pss.used(ev2)
                        s_m1[rp].produced(ev2)
                        s_cslc[kgp].used(ev2)
                        ev4 = st("G", lambda g, fc=fc, rp=rp, stg=stg:
                                 g.tensor_add(stg[:, fc, :], m1[rp], rot[rp]),
                                 waits=cons_waits(s_m1[rp], s_rot[rp]) +
                                 (prod_waits(sstg) if fc == 0 else []))
                        s_m1[rp].used(ev4)
                        s_rot[rp].used(ev4)
                        if fc == NH - 1:
                            sstg.produced(ev4)
                        elif fc == 0:
                            sstg.users = []

                dst = k_scr if which == "k" else v_scr
                sname = which
                for half in range(2):
                    def fn(q, dst=dst, half=half, stg=stg, kg=kg):
                        dap = bass.AP(dst.tensor,
                                      half * 8 * 128 * SEQ + kg * 512,
                                      [[SEQ, 128], [128 * SEQ, 8], [1, 512]])
                        return q.dma_start(dap, stg[:, half * 8:half * 8 + 8, :])
                    ev = st("Q", fn, waits=cons_waits(sstg), dma="dQ")
                    sstg.used(ev)
                    scr_slot[sname].produced(ev)
                yield

        def run_all(g):
            for _ in g:
                pass

        run_all(front(0))
        for kg in range(NKG):
            if kg + 1 < NKG:
                run_all(front(kg + 1))
            run_all(back(kg))

        plan.barrier()

    if "B" in PHASES:
        cf = Carver(arF, ARF_U)
        rotB = [cf.take([512]) for _ in range(2)]
        m1B = [cf.take([512]) for _ in range(2)]
        qf = [cf.take([512]) for _ in range(2)]
        cslq = [cf.take([512]) for _ in range(2)]
        sslq = [cf.take([512]) for _ in range(2)]
        cb = Carver(arB, ARB_U)
        xnfmB2 = [cb.take([NH, 512]) for _ in range(2)]
        qst = cb.take([NH, 512])
        cc = Carver(ctxr, CTX_U)
        wq_s = cc.take([NH, DM])

        s_cslq = [Slot(), Slot()]
        s_sslq = [Slot(), Slot()]
        s_xnfmB2 = [Slot(), Slot()]
        s_wq = Slot()
        s_qst = Slot()
        s_rot = [Slot(), Slot()]
        s_m1 = [Slot(), Slot()]
        s_qf = [Slot(), Slot()]

        def fnw(g):
            sap = bass.AP(wq.tensor, 0,
                          [[DM, 128], [128 * DM, NH], [1, DM]])
            return g.dma_start(wq_s, sap)
        s_wq.produced(st("G", fnw, dma="dG"))

        def bload(r):
            rp = r % 2
            ev = st("S", lambda s, r=r, rp=rp: s.dma_start(
                cslq[rp], cq_scr[:, r * 512:(r + 1) * 512]),
                waits=prod_waits(s_cslq[rp]), dma="dA")
            s_cslq[rp].produced(ev)
            ev = st("S", lambda s, r=r, rp=rp: s.dma_start(
                sslq[rp], sq_scr2[:, r * 512:(r + 1) * 512]),
                waits=prod_waits(s_sslq[rp]), dma="dA")
            s_sslq[rp].produced(ev)
            evs = []
            for j in range(4):
                tt = 4 * r + j

                def fn(g, tt=tt, j=j, rp=rp):
                    with g.register(f"rb{tt}") as rg:
                        g.reg_load(rg, idx_s[0:1, 32 + tt:32 + tt + 1])
                        sap = bass.AP(xnf_scr.tensor, rg,
                                      [[SEQ, 128], [128 * SEQ, NH], [1, 128]])
                        return g.dma_start(
                            xnfmB2[rp][:, :, j * 128:(j + 1) * 128], sap)
                ev = st("G", fn, waits=(prod_waits(s_xnfmB2[rp])
                                        if j == 0 else [])
                        + cons_waits(scr_slot["xnf"]) + [ev_idx], dma="dG")
                evs.append(ev)
                scr_slot["xnf"].used(ev)
            s_xnfmB2[rp].produced(evs)

        bload(0)
        for r in range(NOG):
            rp = r % 2
            xnfmB = xnfmB2[rp]
            s_xnfm = s_xnfmB2[rp]
            if r + 1 < NOG:
                bload(r + 1)

            for oc in range(NH):
                bank = oc % 2
                pss, psa = ps_slot[bank], ps_ap[bank]

                def fn(t, oc=oc, psa=psa, xnfmB=xnfmB):
                    rr = None
                    for fc in range(NH):
                        rr = t.matmul(psa,
                                      wq_s[:, fc, oc * 128:(oc + 1) * 128],
                                      xnfmB[:, fc, :],
                                      start=(fc == 0), stop=(fc == NH - 1))
                    return rr
                ev = st("T", fn, waits=prod_waits(pss) +
                        cons_waits(s_xnfm, s_wq))
                pss.produced(ev)
                s_xnfm.used(ev)
                s_wq.used(ev)

                rp = oc % 2
                ev0 = st("S", lambda s, oc=oc, psa=psa, rp=rp: s.activation(
                    qf[rp], psa, ACT.Identity, bias=bq_s[:, oc:oc + 1],
                    scale=1.0),
                    waits=cons_waits(pss) + prod_waits(s_qf[rp]) + [ev_bias])
                pss.used(ev0)
                s_qf[rp].produced(ev0)
                cslice = cslq[r % 2]
                sslice = sslq[r % 2]
                ev1 = st("V", lambda v, rp=rp: (
                    v.tensor_copy(rotB[rp][0:64, :], qf[rp][64:128, :]),
                    v.tensor_copy(rotB[rp][64:128, :], qf[rp][0:64, :]))[-1],
                    waits=cons_waits(s_qf[rp]) + prod_waits(s_rot[rp]))
                s_rot[rp].produced(ev1)
                ev2 = st("V", lambda v, rp=rp, c=cslice:
                         v.tensor_mul(m1B[rp], qf[rp], c),
                         waits=cons_waits(s_qf[rp], s_cslq[r % 2]) +
                         prod_waits(s_m1[rp]))
                s_qf[rp].used(ev1)
                s_qf[rp].used(ev2)
                s_m1[rp].produced(ev2)
                s_cslq[r % 2].used(ev2)
                ev3 = st("G", lambda g, rp=rp, s=sslice:
                         g.tensor_mul(rotB[rp], rotB[rp], s),
                         waits=cons_waits(s_rot[rp], s_sslq[r % 2]))
                s_rot[rp].produced(ev3)
                s_sslq[r % 2].used(ev3)
                ev4 = st("V", lambda v, oc=oc, rp=rp:
                         v.tensor_add(qst[:, oc, :], m1B[rp], rotB[rp]),
                         waits=cons_waits(s_m1[rp], s_rot[rp]) +
                         (prod_waits(s_qst) if oc == 0 else []))
                s_m1[rp].used(ev4)
                s_rot[rp].used(ev4)
                if oc == NH - 1:
                    s_qst.produced(ev4)
                elif oc == 0:
                    s_qst.users = []

            for half in range(2):
                def fn(q, half=half, r=r):
                    dap = bass.AP(q_scr.tensor,
                                  half * 8 * 128 * TOK + r * 512,
                                  [[TOK, 128], [128 * TOK, 8], [1, 512]])
                    return q.dma_start(dap, qst[:, half * 8:half * 8 + 8, :])
                ev = st("Q", fn, waits=cons_waits(s_qst), dma="dQ")
                s_qst.used(ev)
                scr_slot["q"].produced(ev)

        plan.barrier()

    if "C" in PHASES:
        ccv = Carver(arB, ARB_U)
        k_s = [ccv.take([SEQ]) for _ in range(2)]
        v_t = [ccv.take([32, 128]) for _ in range(2)]
        q_s = [ccv.take([TOK]) for _ in range(2)]
        P_b = [ccv.take([SEQ]) for _ in range(3)]
        PT_b = [ccv.take([32, 128]) for _ in range(3)]
        cfv = Carver(arF, ARF_U)
        mks_s = cfv.take([NH, 256])
        ctx = ar_view(ctxr, 0, [NH, DM])

        s_k = [Slot(), Slot()]
        s_v = [Slot(), Slot()]
        s_q = [Slot(), Slot()]
        s_P = [Slot() for _ in range(3)]
        s_PT = [Slot() for _ in range(3)]
        s_m = Slot()
        s_ctx = Slot()
        s_lac = [Slot() for _ in range(4)]
        s_rl = [Slot() for _ in range(4)]

        s_m.produced(st("Q", lambda q: q.dma_start(mks_s, mks), dma="dQ"))

        def head_loads(h):
            hp = h % 2
            ev = st("S", lambda s, h=h, hp=hp: s.dma_start(
                k_s[hp], k_scr[h * 128:(h + 1) * 128, :]),
                waits=prod_waits(s_k[hp]) + cons_waits(scr_slot["k"]),
                dma="dA")
            s_k[hp].produced(ev)
            ev = st("Q", lambda s, h=h, hp=hp: s.dma_start(
                v_t[hp], v_scr[h * 128:(h + 1) * 128, :], transpose=True),
                waits=prod_waits(s_v[hp]) + cons_waits(scr_slot["v"]),
                dma="dQ", xpose=True)
            s_v[hp].produced(ev)
            ev = st("S", lambda s, h=h, hp=hp: s.dma_start(
                q_s[hp], q_scr[h * 128:(h + 1) * 128, :]),
                waits=prod_waits(s_q[hp]) + cons_waits(scr_slot["q"]),
                dma="dA")
            s_q[hp].produced(ev)

        head_loads(0)
        pend = []

        def emit_ctx(p):
            (par, hp, W, tt, h) = p
            cbank = 4 + (len(pend) + h + tt) % 2
            cps, cpa = ps_slot[cbank], ps_ap[cbank]

            def fn(t, par=par, hp=hp, W=W, cpa=cpa):
                r = None
                for kc in range(W // 128):
                    r = t.matmul(cpa[:, 0:128], PT_b[par][:, kc, :],
                                 v_t[hp][:, kc, :],
                                 start=(kc == 0), stop=(kc == W // 128 - 1))
                return r
            ev_c = st("T", fn, waits=prod_waits(cps) +
                      cons_waits(s_PT[par], s_v[hp]))
            cps.produced(ev_c)
            s_PT[par].used(ev_c)
            s_v[hp].used(ev_c)

            ev_e = st("V", lambda v, tt=tt, h=h, par=par, cpa=cpa:
                      v.tensor_scalar(ctx[:, tt, h * 128:(h + 1) * 128],
                                      cpa[:, 0:128],
                                      rl_s[:, par:par + 1], None,
                                      op0=ALU.mult),
                      waits=cons_waits(cps, s_rl[par]))
            cps.used(ev_e)
            s_rl[par].used(ev_e)
            s_ctx.produced(ev_e)

        pancnt = 0
        for h in range(NH):
            hp = h % 2

            for tt in range(16):
                if tt == 5 and h + 1 < NH:
                    head_loads(h + 1)
                par = pancnt % 3
                pancnt += 1
                W = wpad(tt)
                nch = (W + 511) // 512
                exp_evs = []
                ev_nc = None
                for ci in range(nch):
                    n0 = ci * 512
                    N = min(512, W - n0)
                    bank = (ci + pancnt * 4) % 4
                    pss, psa = ps_slot[bank], ps_ap[bank]
                    ev_mm = st("T", lambda t, tt=tt, hp=hp, n0=n0, N=N, psa=psa:
                               t.matmul(psa[:, 0:N],
                                        q_s[hp][:, tt * 128:(tt + 1) * 128],
                                        k_s[hp][:, n0:n0 + N],
                                        start=True, stop=True),
                               waits=prod_waits(pss) +
                               cons_waits(s_q[hp], s_k[hp]))
                    pss.produced(ev_mm)
                    s_q[hp].used(ev_mm)
                    s_k[hp].used(ev_mm)
                    ev_ready = ev_mm
                    if ci == nch - 1:
                        moff = (W - 256) - n0
                        assert moff >= 0
                        ev_ready = st("V", lambda v, psa=psa, moff=moff, tt=tt:
                                      v.tensor_add(psa[:, moff:moff + 256],
                                                   psa[:, moff:moff + 256],
                                                   mks_s[:, tt, :]),
                                      waits=[ev_mm] + cons_waits(s_m))
                        pss.produced(ev_ready)
                    if ci == 0:
                        N0 = N
                        ev_nc = st("V", lambda v, psa=psa, N0=N0, par=par:
                                   v.reduce_max(negc[:, par:par + 1],
                                                psa[:, 0:N0],
                                                axis=AX, negate=True),
                                   waits=[ev_ready] + prod_waits(s_lac[par]))
                        pss.used(ev_nc)
                    ev = st("S", lambda s, psa=psa, n0=n0, N=N, par=par, ci=ci:
                            s.activation(P_b[par][:, n0:n0 + N], psa[:, 0:N],
                                         ACT.Exp, bias=negc[:, par:par + 1],
                                         scale=1.0,
                                         accum_out=lacc[:, par * 8 + ci:
                                                        par * 8 + ci + 1]),
                            waits=[ev_ready, ev_nc] +
                            (prod_waits(s_P[par]) if ci == 0 else []))
                    pss.used(ev)
                    exp_evs.append(ev)
                s_P[par].produced(exp_evs[-1])
                s_lac[par].produced(exp_evs[-1])

                ev_l = st("V", lambda v, par=par, nch=nch: v.reduce_sum(
                    lsum[:, par:par + 1], lacc[:, par * 8:par * 8 + nch],
                    axis=AX),
                    waits=exp_evs)
                ev_rl = st("V", lambda v, par=par: v.reciprocal(
                    rl_s[:, par:par + 1], lsum[:, par:par + 1]),
                    waits=[ev_l] + prod_waits(s_rl[par]))
                s_rl[par].produced(ev_rl)
                s_lac[par].used(ev_l)

                ev_t = st("Q", lambda q, par=par, W=W: q.dma_start(
                    PT_b[par][:, 0:W // 128, :], P_b[par][:, 0:W],
                    transpose=True),
                    waits=cons_waits(s_P[par]) + prod_waits(s_PT[par]),
                    dma="dQ", xpose=True)
                s_P[par].used(ev_t)
                s_PT[par].produced(ev_t)

                pend.append((par, hp, W, tt, h))
                if len(pend) > 2:
                    emit_ctx(pend.pop(0))

        while pend:
            emit_ctx(pend.pop(0))

        plan.barrier()

    if "D" in PHASES:
        cd = Carver(arB, ARB_U)
        ctxfm2 = [cd.take([NH, 512]), ar_view(ctxr, 0, [NH, 512])]
        wo_all = cd.take([NH, DM])
        x1nt = cd.take([DM])
        cfd = Carver(arF, ARF_U)
        att2 = [cfd.take([DM]) for _ in range(2)]
        xtd = [cfd.take([DM]) for _ in range(2)]
        tmpd = cfd.take([DM])
        bo_s = cfd.take([DM])

        s_cfm2 = [Slot(), Slot()]
        s_wo = Slot()
        s_att2 = [Slot(), Slot()]
        s_xt = [Slot(), Slot()]
        s_x1 = Slot()
        s_tmp = Slot()
        s_stats = Slot()
        s_bo = Slot()

        s_bo.produced(st("Q", lambda q: q.dma_start(bo_s, bob), dma="dQ"))
        woevs = []
        for og in range(4):
            def fnw(s, og=og):
                sap = bass.AP(wo_scr.tensor, og * 512,
                              [[DM, 128], [128 * DM, NH], [1, 512]])
                return s.dma_start(wo_all[:, :, og * 512:(og + 1) * 512], sap)
            woevs.append(st("S", fnw, dma="dA"))
        s_wo.produced(woevs)

        def cfm_load(ttb):
            bp = ttb % 2
            tevs = []
            for j in range(4):
                tt = ttb * 4 + j
                ev = st("Q", lambda q, tt=tt, j=j, bp=bp: q.dma_start(
                    ctxfm2[bp][:, :, j * 128:(j + 1) * 128],
                    ar_view(ctxr, 0, [NH, DM])[:, tt, :],
                    transpose=True),
                    waits=cons_waits(s_ctx) +
                    (prod_waits(s_cfm2[bp]) if j == 0 else []), dma="dQ",
                    xpose=True)
                tevs.append(ev)
                s_ctx.used(ev)
            s_cfm2[bp].produced(tevs)

        cfm_load(0)
        s_cfm2[1].users = list(s_cfm2[0].ready)
        cfm_load(1)
        for tt in range(16):
            ttb = tt // 4
            ctxfm = ctxfm2[ttb % 2]
            s_cfm = s_cfm2[ttb % 2]
            tp = tt % 2
            att = att2[tp]
            s_att = s_att2[tp]
            j4 = tt % 4
            for og in range(4):
                bank = og
                pss, psa = ps_slot[bank], ps_ap[bank]

                def fn(t, j4=j4, og=og, psa=psa, ctxfm=ctxfm):
                    r = None
                    for fc in range(NH):
                        r = t.matmul(psa,
                                     ctxfm[:, fc, j4 * 128:(j4 + 1) * 128],
                                     wo_all[:, fc, og * 512:(og + 1) * 512],
                                     start=(fc == 0), stop=(fc == NH - 1))
                    return r
                ev = st("T", fn, waits=prod_waits(pss) +
                        cons_waits(s_cfm, s_wo))
                pss.produced(ev)
                s_cfm.used(ev)
                s_wo.used(ev)

                ev = st("V", lambda v, og=og, psa=psa, att=att:
                        v.tensor_add(att[:, og * 512:(og + 1) * 512],
                                     psa, bo_s[:, og * 512:(og + 1) * 512]),
                        waits=cons_waits(pss, s_bo) +
                        (prod_waits(s_att) if og == 0 else []))
                pss.used(ev)
                if og == 3:
                    s_att.produced(ev)
                elif og == 0:
                    s_att.users = []
            if j4 == 3 and ttb + 2 < 4:
                cfm_load(ttb + 2)

            xp = tt % 2

            def fnx(q, tt=tt, xp=xp):
                with q.register(f"rx{tt}") as r:
                    q.reg_load(r, idx_s[0:1, 16 + tt:16 + tt + 1])
                    src = bass.AP(xin.tensor, r, [[DM, 128], [1, DM]])
                    return q.dma_start(xtd[xp], src)
            ev = st("Q", fnx, waits=prod_waits(s_xt[xp]) + [ev_idx],
                    dma="dQ")
            s_xt[xp].produced(ev)

            def ln_soup(src_ap, dst_ap, sL, sD, extra):
                ssum = st_s[:, 0:1]
                nmean = st_s[:, 1:2]
                ssq = st_s[:, 2:3]
                var = st_s[:, 3:4]
                sdv = st_s[:, 4:5]
                rstd = st_s[:, 5:6]
                t1 = st_s[:, 6:7]
                nmr = st_s[:, 7:8]
                e1 = st("V", lambda v: v.reduce_sum(ssum, src_ap, axis=AX),
                        waits=cons_waits(sL) + prod_waits(s_stats) + extra)
                e2 = st("S", lambda s: s.activation(
                    sqs, src_ap, ACT.Square, accum_out=ssq),
                    waits=cons_waits(sL) + [e1] + extra)
                e3 = st("V", lambda v: v.tensor_scalar(
                    nmean, ssum, -1.0 / DM, None, op0=ALU.mult),
                    waits=[e1])
                e4 = st("V", lambda v: v.tensor_scalar(
                    var, ssq, 1.0 / DM, None, op0=ALU.mult), waits=[e2])
                e5 = st("G", lambda g: g.tensor_mul(
                    t1, nmean, nmean), waits=[e3])
                e6 = st("G", lambda g: g.tensor_sub(var, var, t1),
                        waits=[e4, e5])
                e7 = st("S", lambda s: s.activation(
                    sdv, var, ACT.Sqrt, bias=eps_s, scale=1.0), waits=[e6])
                e8 = st("V", lambda v: v.reciprocal(rstd, sdv), waits=[e7])
                e8b = st("G", lambda g: g.tensor_mul(nmr, nmean, rstd),
                         waits=[e8])
                e9 = st("S", lambda s, src_ap=src_ap, dst_ap=dst_ap:
                        s.activation(dst_ap, src_ap, ACT.Identity,
                                     bias=nmr, scale=rstd),
                        waits=[e8b] + prod_waits(sD))
                sL.used(e9)
                sD.produced(e9)
                s_stats.produced(e9)
                s_stats.used(e9)
                return e9

            eln2 = ln_soup(att, tmpd, s_att, s_tmp, [])
            ev_x1 = st("G", lambda g, xp=xp: g.tensor_add(
                tmpd, tmpd, xtd[xp]),
                waits=cons_waits(s_tmp, s_xt[xp]) + prod_waits(s_x1))
            s_tmp.used(ev_x1)
            s_xt[xp].used(ev_x1)
            s_x1.produced(ev_x1)
            ev_w1 = st("S", lambda s, tt=tt: s.dma_start(
                x1_scr[tt * 128:(tt + 1) * 128, :], tmpd),
                waits=cons_waits(s_x1), dma="dA")
            s_x1.used(ev_w1)
            scr_slot["x1"].produced(ev_w1)
            eln3 = ln_soup(tmpd, x1nt, s_x1, s_tmp, [ev_w1])
            ev_w2 = st("S", lambda s, tt=tt: s.dma_start(
                x1n_scr[tt * 128:(tt + 1) * 128, :], x1nt),
                waits=cons_waits(s_tmp), dma="dA")
            s_tmp.used(ev_w2)
            scr_slot["x1n"].produced(ev_w2)

        plan.barrier()

    if "E" in PHASES:
        ce = Carver(arB, ARB_U)
        wg_s = [ce.take([NH, 256]) for _ in range(2)]
        wu_s = [ce.take([NH, 256]) for _ in range(2)]
        wd_s = [ce.take([8, 512]) for _ in range(2)]
        x1nfm2 = [ce.take([NH, 512]) for _ in range(2)]
        g_sb = ce.take([512])
        u_sb = ce.take([512])
        cc = Carver(ctxr, CTX_U)
        f_b = cc.take([64, 512])
        cfe = Carver(arF, ARF_U)
        dout = cfe.take([4, DM])
        x1te = cfe.take([DM])
        sd_sl = [cfe.take([512]) for _ in range(2)]

        s_wg = [Slot(), Slot()]
        s_wu = [Slot(), Slot()]
        s_fA, s_fB = Slot(), Slot()
        s_wd = [Slot(), Slot()]
        s_gs, s_us = Slot(), Slot()
        s_xnf2 = [Slot(), Slot()]
        s_do = Slot()
        s_x1t = Slot()
        s_tmp = Slot()
        s_ot = Slot()
        s_stats = Slot()
        s_sdb = [Slot(), Slot()]

        def x1nfm_load(r):
            rp = r % 2
            tevs = []
            for j in range(4):
                tt = r * 4 + j
                ev = st("Q", lambda q, tt=tt, j=j, rp=rp: q.dma_start(
                    x1nfm2[rp][:, :, j * 128:(j + 1) * 128],
                    x1n_scr[tt * 128:(tt + 1) * 128, :], transpose=True),
                    waits=cons_waits(scr_slot["x1n"]) +
                    (prod_waits(s_xnf2[rp]) if j == 0 else []), dma="dQ",
                    xpose=True)
                tevs.append(ev)
                scr_slot["x1n"].used(ev)
            s_xnf2[rp].produced(tevs)

        x1nfm_load(0)
        for r in range(NOG):
            rp = r % 2
            x1nfm = x1nfm2[rp]
            s_xnf = s_xnf2[rp]
            if r + 1 < NOG:
                x1nfm_load(r + 1)

            for ic in range(64):
                sl = ic // 2
                slp = sl % 2
                icw = ic % 2
                if icw == 0:
                    for wsl, wptr, w8 in ((s_wg[slp], wg_s[slp], wg8),
                                          (s_wu[slp], wu_s[slp], wu8)):
                        def fnw(g, wptr=wptr, w8=w8, sl=sl):
                            sap = bass.AP(w8.tensor, sl * 256,
                                          [[ITER, 128], [128 * ITER, NH],
                                           [1, 256]])
                            return g.dma_start(wptr, sap)
                        ev = st("G", fnw, waits=prod_waits(wsl), dma="dG")
                        wsl.produced(ev)

                gb, ub = ps_slot[ic % 2], ps_slot[2 + ic % 2]
                ga, ua = ps_ap[ic % 2], ps_ap[2 + ic % 2]

                def fng(t, icw=icw, slp=slp, ga=ga, x1nfm=x1nfm):
                    rr = None
                    for fc in range(NH):
                        rr = t.matmul(ga, wg_s[slp][:, fc,
                                                    icw * 128:(icw + 1) * 128],
                                      x1nfm[:, fc, :],
                                      start=(fc == 0), stop=(fc == NH - 1))
                    return rr
                evg = st("T", fng, waits=prod_waits(gb) +
                         cons_waits(s_wg[slp], s_xnf))
                gb.produced(evg)
                s_wg[slp].used(evg)
                s_xnf.used(evg)

                def fnu(t, icw=icw, slp=slp, ua=ua, x1nfm=x1nfm):
                    rr = None
                    for fc in range(NH):
                        rr = t.matmul(ua, wu_s[slp][:, fc,
                                                    icw * 128:(icw + 1) * 128],
                                      x1nfm[:, fc, :],
                                      start=(fc == 0), stop=(fc == NH - 1))
                    return rr
                evu = st("T", fnu, waits=prod_waits(ub) +
                         cons_waits(s_wu[slp], s_xnf))
                ub.produced(evu)
                s_wu[slp].used(evu)
                s_xnf.used(evu)

                ev1 = st("S", lambda s, ic=ic, ga=ga: s.activation(
                    g_sb, ga, ACT.Silu, bias=0.0, scale=sg_s[:, ic:ic + 1]),
                    waits=cons_waits(gb) + prod_waits(s_gs) + [ev_bias])
                gb.used(ev1)
                s_gs.produced(ev1)
                ev2 = st("V", lambda v, ic=ic, ua=ua: v.tensor_scalar(
                    u_sb, ua, su_s[:, ic:ic + 1], None, op0=ALU.mult),
                    waits=cons_waits(ub) + prod_waits(s_us) + [ev_bias])
                ub.used(ev2)
                s_us.produced(ev2)
                s_f = s_fA if ic < 32 else s_fB
                ev3 = st("V", lambda v, ic=ic: v.tensor_mul(
                    f_b[:, ic, :], g_sb, u_sb),
                    waits=cons_waits(s_gs, s_us) +
                    (prod_waits(s_f) if ic in (0, 32) else []))
                s_gs.used(ev3)
                s_us.used(ev3)
                if ic == 31:
                    s_fA.produced(ev3)
                elif ic == 63:
                    s_fB.produced(ev3)
                elif ic in (0, 32):
                    s_f.users = []

            for og in range(4):
                ogp = og % 2
                ev = st("S", lambda s, og=og, ogp=ogp: s.dma_start(
                    sd_sl[ogp], sdb[:, og * 512:(og + 1) * 512]),
                    waits=prod_waits(s_sdb[ogp]), dma="dA")
                s_sdb[ogp].produced(ev)
                for quar in range(8):
                    qp = quar % 2
                    def fnw(g, og=og, quar=quar, qp=qp):
                        sap = bass.AP(wd8.tensor,
                                      quar * 8 * 128 * DM + og * 512,
                                      [[DM, 128], [128 * DM, 8], [1, 512]])
                        return g.dma_start(wd_s[qp], sap)
                    ev = st("G", fnw, waits=prod_waits(s_wd[qp]), dma="dG")
                    s_wd[qp].produced(ev)

                    for j in range(4):
                        bank = 4 + j
                        pss, psa = ps_slot[bank], ps_ap[bank]

                        def fn(t, j=j, quar=quar, qp=qp, psa=psa):
                            rr = None
                            for ic in range(8):
                                rr = t.matmul(
                                    psa, f_b[:, quar * 8 + ic,
                                             j * 128:(j + 1) * 128],
                                    wd_s[qp][:, ic, :],
                                    start=(quar == 0 and ic == 0),
                                    stop=(quar == 7 and ic == 7))
                            return rr
                        s_f = s_fA if quar < 4 else s_fB
                        ev = st("T", fn, waits=(prod_waits(pss)
                                                if quar == 0 else []) +
                                cons_waits(s_f, s_wd[qp]))
                        if quar == 0:
                            pss.produced(ev)
                        else:
                            pss.ready = [ev]
                        s_f.used(ev)
                        s_wd[qp].used(ev)

                        if quar == 7:
                            ev = st("V", lambda v, j=j, og=og, psa=psa,
                                    ogp=ogp:
                                    v.tensor_mul(
                                        dout[:, j, og * 512:(og + 1) * 512],
                                        psa, sd_sl[ogp]),
                                    waits=cons_waits(pss, s_sdb[ogp]) +
                                    (prod_waits(s_do)
                                     if og == 0 and j == 0 else []))
                            pss.used(ev)
                            s_sdb[ogp].used(ev)
                            if og == 3 and j == 3:
                                s_do.produced(ev)
                            elif og == 0 and j == 0:
                                s_do.users = []

            for j in range(4):
                tt = r * 4 + j
                dtile = dout[:, j, :]
                ev_x1 = st("S", lambda s, tt=tt: s.dma_start(
                    x1te, x1_scr[tt * 128:(tt + 1) * 128, :]),
                    waits=prod_waits(s_x1t) + cons_waits(scr_slot["x1"]),
                    dma="dA")
                s_x1t.produced(ev_x1)

                ssum = st_s[:, 0:1]
                nmean = st_s[:, 1:2]
                ssq = st_s[:, 2:3]
                var = st_s[:, 3:4]
                sdv = st_s[:, 4:5]
                rstd = st_s[:, 5:6]
                t1 = st_s[:, 6:7]
                e1 = st("V", lambda v, dtile=dtile: v.reduce_sum(
                    ssum, dtile, axis=AX),
                    waits=cons_waits(s_do) + prod_waits(s_stats))
                e2 = st("S", lambda s, dtile=dtile: s.activation(
                    sqs, dtile, ACT.Square, accum_out=ssq),
                    waits=cons_waits(s_do) + [e1])
                e3 = st("V", lambda v: v.tensor_scalar(
                    nmean, ssum, -1.0 / DM, None, op0=ALU.mult), waits=[e1])
                e4 = st("V", lambda v: v.tensor_scalar(
                    var, ssq, 1.0 / DM, None, op0=ALU.mult), waits=[e2])
                e5 = st("G", lambda g: g.tensor_mul(
                    t1, nmean, nmean), waits=[e3])
                e6 = st("G", lambda g: g.tensor_sub(var, var, t1),
                        waits=[e4, e5])
                e7 = st("S", lambda s: s.activation(
                    sdv, var, ACT.Sqrt, bias=eps_s, scale=1.0), waits=[e6])
                e8 = st("V", lambda v: v.reciprocal(rstd, sdv), waits=[e7])
                e9 = st("V", lambda v, dtile=dtile: v.tensor_scalar(
                    dtile, dtile, nmean, rstd, op0=ALU.add, op1=ALU.mult),
                    waits=[e8] + prod_waits(s_tmp))
                s_do.used(e9)
                s_tmp.produced(e9)
                s_stats.produced(e8)
                s_stats.used(e9)
                ev_o = st("V", lambda v, dtile=dtile: v.tensor_add(
                    dtile, dtile, x1te),
                    waits=cons_waits(s_tmp, s_x1t) + [e9])
                s_tmp.used(ev_o)
                s_x1t.used(ev_o)
                s_do.used(ev_o)
                s_ot.produced(ev_o)
                ev_w = st("S", lambda s, tt=tt, dtile=dtile: s.dma_start(
                    out[tt * 128:(tt + 1) * 128, :], dtile),
                    waits=cons_waits(s_ot), dma="dA")
                s_ot.used(ev_w)
                s_do.used(ev_w)

        plan.barrier()

    with nc.Block() as block:
        block.tensor(lambda t: plan.replay(t, "T", sems))
        block.vector(lambda v: plan.replay(v, "V", sems))
        block.scalar(lambda s: plan.replay(s, "S", sems))
        block.gpsimd(lambda g: plan.replay(g, "G", sems))
        block.sync(lambda q: plan.replay(q, "Q", sems))

    stack.close()
    return nc


def _f32_trig(pos):
    import jax
    import jax.numpy as jnp
    try:
        dev = jax.local_devices(backend="cpu")[0]
    except Exception:
        dev = None
    import contextlib
    cmgr = jax.default_device(dev) if dev is not None else contextlib.nullcontext()
    with cmgr:
        inv = 1.0 / 10000.0 ** (jnp.arange(0, HD, 2, dtype=jnp.float32) / HD)
        t = jnp.asarray(pos, dtype=jnp.float32)
        fr = t[None, :] * inv[:, None]
        c = np.asarray(jnp.cos(fr)).astype(np.float64)
        s = np.asarray(jnp.sin(fr)).astype(np.float64)
    return c, s


def _tables(own_pos):
    def mk(pos, scale):
        c, s = _f32_trig(pos)
        c = c * scale
        s = s * scale
        cosf = np.concatenate([c, c], axis=0)
        sinf = np.concatenate([-s, s], axis=0)
        out = []
        for tab in (cosf, sinf):
            hi = tab.astype(np.float32)
            lo = (tab - hi.astype(np.float64)).astype(np.float32)
            out += [np.ascontiguousarray(hi), np.ascontiguousarray(lo)]
        return out

    kk = mk(np.arange(SEQ), 1.0)
    scl = 1.0 / np.float64(np.float32(np.sqrt(np.float64(HD))))
    qq = mk(own_pos, scl)
    return kk, qq


def _masks(qbtab):
    tri = np.triu(np.full((128, 128), NEG, dtype=np.float32), k=1)
    m = np.zeros((16, 128, 256), dtype=np.float32)
    for tt, qb in enumerate(qbtab):
        wp = wpad(tt)
        we = 128 * (qb + 1)
        if we == wp:
            m[tt, :, 128:256] = tri
        else:
            assert we == wp - 128
            m[tt, :, 0:128] = tri
            m[tt, :, 128:256] = NEG
    return np.ascontiguousarray(
        m.transpose(1, 0, 2).reshape(128, 16 * 256).astype(np.float32))


def kernel(L0_attn_x, L0_attn_Wq, L0_attn_bq, L0_attn_Wdown, L0_attn_Wk_up,
           L0_attn_Wv_up, L0_attn_Wo, L0_attn_bo, L0_ffn_w_gate, L0_ffn_w_up,
           L0_ffn_w_down, L0_ffn_s_gate, L0_ffn_s_up, L0_ffn_s_down):
    x = np.asarray(L0_attn_x)
    fdt = x.dtype
    asf = lambda a: np.ascontiguousarray(np.asarray(a))

    bq_l = asf(np.asarray(L0_attn_bq).reshape(NH, 128).T)
    wk = np.asarray(L0_attn_Wk_up)
    perm = np.concatenate([np.arange(h * 128 + 64, h * 128 + 128).tolist() +
                           np.arange(h * 128, h * 128 + 64).tolist()
                           for h in range(NH)]).astype(np.int64)
    wku_sw = asf(wk[:, perm])
    sg_l = asf(np.asarray(L0_ffn_s_gate).reshape(64, 128).T)
    su_l = asf(np.asarray(L0_ffn_s_up).reshape(64, 128).T)
    bo_b = asf(np.broadcast_to(np.asarray(L0_attn_bo)[None, :], (128, DM)))
    sd_b = asf(np.broadcast_to(np.asarray(L0_ffn_s_down)[None, :], (128, DM)))

    in_maps = []
    reasm = []
    for c in range(8):
        b, p = c // 2, c % 2
        owns = (0, 3) if p == 0 else (1, 2)
        qbtab = [qb for qb in range(32) if qb % 4 in owns]
        own_pos = np.concatenate(
            [np.arange(qb * 128, qb * 128 + 128) for qb in qbtab])
        reasm.append((b, qbtab))
        kk, qq = _tables(own_pos)
        idx = np.zeros(64, dtype=np.int32)
        for tt in range(16):
            idx[16 + tt] = qbtab[tt] * 128 * DM
            idx[32 + tt] = qbtab[tt] * 128
        in_maps.append({
            "xin": asf(x[b]),
            "wq": asf(L0_attn_Wq), "wdn": asf(L0_attn_Wdown),
            "wku": asf(L0_attn_Wk_up), "wkus": wku_sw, "wvu": asf(L0_attn_Wv_up),
            "wo": asf(L0_attn_Wo),
            "wg8": asf(L0_ffn_w_gate), "wu8": asf(L0_ffn_w_up),
            "wd8": asf(L0_ffn_w_down),
            "bqv": bq_l, "bob": bo_b, "sgv": sg_l, "suv": su_l, "sdb": sd_b,
            "ckh": kk[0], "ckl": kk[1], "skh": kk[2], "skl": kk[3],
            "cqh": qq[0], "cql": qq[1], "sqh": qq[2], "sql": qq[3],
            "mks": _masks(qbtab),
            "idx": idx.reshape(1, 64),
        })

    nc = build_program()
    res = run_bass_kernel_spmd(nc, in_maps, core_ids=list(range(8)))

    odt = np.asarray(res.results[0]["out"]).dtype
    outp = np.zeros((NB, SEQ, DM), dtype=odt)
    for c in range(8):
        b, qbtab = reasm[c]
        o = np.asarray(res.results[c]["out"])
        for tt, qb in enumerate(qbtab):
            outp[b, qb * 128:(qb + 1) * 128, :] = o[tt * 128:(tt + 1) * 128, :]
    return outp
